# version 03
# baseline (speedup 1.0000x reference)
import numpy as np
from contextlib import ExitStack
import concourse.bass as bass
import concourse.mybir as mybir
from concourse.bass_utils import run_bass_kernel_spmd

F32 = mybir.dt.float32
BF16 = mybir.dt.bfloat16
I16 = mybir.dt.int16
I32 = mybir.dt.int32
ALU = mybir.AluOpType
AF = mybir.ActivationFunctionType
AX = mybir.AxisListType

MAGIC = 12582912.0
D = 1024
DI = 3072
T = 4096
NDK = D // 128
LN_EPS = 1e-5
DEN_EPS = 1e-6
CA = 512
NCA = T // CA
GRP_ALL = [list(range(8))]
GRP_PAIR = [[0, 1], [2, 3], [4, 5], [6, 7]]

SL_RS1, SL_S1 = 0, 1
SL_RS2, SL_S2 = 2, 3
SL_RS3, SL_S3 = 4, 5
SL_C4, SL_S4 = 6, 7
SL_C5, SL_S5 = 8, 9
SL_HINV = 10


class P:

    def __init__(self, nc, stack):
        self.nc = nc
        self.stack = stack
        self.sems = {}
        self.val = {}
        for name in ("qin", "qout", "T", "V", "S", "G", "CC", "BOOT"):
            self.sems[name] = stack.enter_context(nc.semaphore(name))
            self.val[name] = 0

    def inc(self, handle, sem, n=1):
        handle.then_inc(self.sems[sem], n)
        self.val[sem] += n
        return self.val[sem]

    def wait(self, eng, sem, v=None):
        eng.wait_ge(self.sems[sem], self.val[sem] if v is None else v)


def build_nc(dbg=False):
    nc = bass.Bass("TRN2", target_bir_lowering=False, debug=False)

    x_in = nc.dram_tensor("x", [T, D], F32, kind="ExternalInput").ap()
    w_qkv = nc.dram_tensor("w_qkv", [D, 3 * D], F32, kind="ExternalInput").ap()
    b_qkv = nc.dram_tensor("b_qkv", [3 * D], F32, kind="ExternalInput").ap()
    w_gate = nc.dram_tensor("w_gate", [D, D], F32, kind="ExternalInput").ap()
    b_gate = nc.dram_tensor("b_gate", [D], F32, kind="ExternalInput").ap()
    w_proj = nc.dram_tensor("w_proj", [D, D], F32, kind="ExternalInput").ap()
    b_proj = nc.dram_tensor("b_proj", [D], F32, kind="ExternalInput").ap()
    ln_g = nc.dram_tensor("ln_g", [D], F32, kind="ExternalInput").ap()
    ln_b = nc.dram_tensor("ln_b", [D], F32, kind="ExternalInput").ap()
    fw_gate = nc.dram_tensor("fw_gate", [DI, D], F32, kind="ExternalInput").ap()
    fw_up = nc.dram_tensor("fw_up", [DI, D], F32, kind="ExternalInput").ap()
    fw_down = nc.dram_tensor("fw_down", [D, DI], F32, kind="ExternalInput").ap()
    hflag = nc.dram_tensor("hflag", [128, 1], F32, kind="ExternalInput").ap()

    y_out = nc.dram_tensor("y", [T, D], F32, kind="ExternalOutput").ap()
    t_slots = nc.dram_tensor("t_slots", [128, 16], F32, kind="ExternalOutput").ap()
    t_macc = nc.dram_tensor("t_macc", [128, 8], F32, kind="ExternalOutput").ap()
    t_cc = nc.dram_tensor("t_cc", [4, 16], F32, kind="ExternalOutput").ap()

    def scratch(name, shape, dtype):
        if dbg == 1:
            return nc.dram_tensor(name, shape, dtype, kind="ExternalOutput").ap()
        return nc.dram_tensor(name, shape, dtype).ap()

    s_xn = scratch("s_xn", [T, D], F32)
    s_qhi = scratch("s_qhi", [D, T], BF16)
    s_qlo = scratch("s_qlo", [D, T], BF16)
    s_khi = scratch("s_khi", [D, T], BF16)
    s_klo = scratch("s_klo", [D, T], BF16)
    s_vhi = scratch("s_vhi", [D, T], BF16)
    s_vlo = scratch("s_vlo", [D, T], BF16)
    s_num = scratch("s_num", [D, T], F32)
    s_den = scratch("s_den", [1, T], F32)
    s_upraw = scratch("s_upraw", [DI, T], F32)
    s_gateraw = scratch("s_gateraw", [DI, T], F32)
    s_upq = scratch("s_upq", [DI, T], I16)
    s_gateq = scratch("s_gateq", [DI, T], I16)
    s_g2q = scratch("s_g2q", [DI, T], I16)
    s_wdthi = scratch("s_wdthi", [D, DI], BF16)
    s_wdtlo = scratch("s_wdtlo", [D, DI], BF16)
    s_r = scratch("s_r", [T, D], F32)
    t_dbg = nc.dram_tensor("t_dbg", [128, 64], F32, kind="ExternalOutput").ap() if dbg else None
    t_pr = {}
    if dbg == 2:
        for nm, shape, dt_ in (("upraw", [128, T], F32), ("upq", [128, T], I16),
                               ("gateq", [128, T], I16), ("g2q", [128, T], I16),
                               ("r", [128, D], F32), ("num", [128, T], F32),
                               ("wdh", [128, DI], BF16)):
            t_pr[nm] = nc.dram_tensor("t_pr_" + nm, shape, dt_, kind="ExternalOutput").ap()
    t_st = nc.dram_tensor("t_st", [T // 128 * 128, 8], F32, kind="ExternalOutput").ap() if dbg else None
    t_S = nc.dram_tensor("t_S", [128, NDK * 1032], F32, kind="ExternalOutput").ap() if dbg else None

    cc1_in = nc.dram_tensor("cc1_in", [1, 16], F32).ap()
    cc1_out = nc.dram_tensor("cc1_out", [1, 16], F32).ap()
    cc23_in = nc.dram_tensor("cc23_in", [1, 16], F32).ap()
    cc23_out = nc.dram_tensor("cc23_out", [1, 16], F32).ap()
    cc4_in = nc.dram_tensor("cc4_in", [1, 16], F32).ap()
    cc4_out = nc.dram_tensor("cc4_out", [1, 16], F32).ap()
    cc5_in = nc.dram_tensor("cc5_in", [1, 16], F32).ap()
    cc5_out = nc.dram_tensor("cc5_out", [1, 16], F32).ap()
    cc_s_in = nc.dram_tensor("cc_s_in", [128, NDK * 1032], F32).ap()
    cc_s_out = nc.dram_tensor("cc_s_out", [128, NDK * 1032], F32).ap()

    with ExitStack() as stack:
        p = P(nc, stack)
        stack.enter_context(nc.Block())
        t, v, s, g, q = nc.tensor, nc.vector, nc.scalar, nc.gpsimd, nc.sync

        def sbuf(st, name, shape, dtype):
            return st.enter_context(nc.sbuf_tensor(name, shape, dtype))

        def psum(st, name, shape, dtype=F32):
            return st.enter_context(nc.psum_tensor(name, shape, dtype))

        sem_nums = [h.num for h in p.sems.values()]
        boot_rng = range(min(sem_nums), max(sem_nums) + 1)
        g.dma_reset(boot_rng)
        g.sem_clear(boot_rng)
        h = g.engine_nop()
        p.inc(h, "BOOT")
        for eng in (t, v, s, q):
            eng.nop(cycle_cnt=30000)
            p.wait(eng, "BOOT", 1)

        ones_r32 = sbuf(stack, "ones_r32", [1, 128], F32)
        ones_c16 = sbuf(stack, "ones_c16", [128, 1], BF16)
        ident32 = sbuf(stack, "ident32", [128, 128], F32)
        iden16 = sbuf(stack, "iden16", [128, 128], BF16)
        iota_f = sbuf(stack, "iota_f", [128, 512], F32)
        pvec_f = sbuf(stack, "pvec_f", [128, 1], F32)
        slots = sbuf(stack, "slots", [128, 16], F32)
        macc = sbuf(stack, "macc", [128, 8], F32)
        hfl = sbuf(stack, "hfl", [128, 1], F32)
        bq_sb = sbuf(stack, "bq_sb", [128, 24], F32)
        bg_sb = sbuf(stack, "bg_sb", [128, 8], F32)
        scrow = sbuf(stack, "scrow", [2, 16], F32)
        epsc = sbuf(stack, "epsc", [128, 1], F32)
        red1 = sbuf(stack, "red1", [2, 1], F32)
        ps_red = psum(stack, "ps_red", [128, 512], F32)
        dbg_sb = sbuf(stack, "dbg_sb", [128, 64], F32)

        def slot(col):
            return slots[:, col:col + 1]

        def fence(h, sem, eng):
            val = p.inc(h, sem)
            p.wait(eng, sem, val)

        def phase_barrier():
            for eng in (t, v, s, g, q):
                for semname in ("T", "V", "S", "G", "qin", "qout", "CC"):
                    if p.val[semname]:
                        p.wait(eng, semname)

        def split2(dst_hi, dst_lo, src, eng=None, sem="V"):
            e = eng or v
            h = e.tensor_copy(dst_hi, src)
            fence(h, sem, e)
            return e.tensor_tensor(dst_lo, src, dst_hi, ALU.subtract)

        def mm3(out_ps, lh_hi, lh_lo, rh_hi, rh_lo, start, stop):
            t.matmul(out_ps, lh_hi, rh_hi, start=start, stop=False)
            t.matmul(out_ps, lh_hi, rh_lo, start=False, stop=False)
            return t.matmul(out_ps, lh_lo, rh_hi, start=False, stop=stop)

        def part_max_to_col(col, maccol):
            p.wait(t, "V")
            h = t.transpose(ps_red[0:1, 0:128], macc[:, maccol:maccol + 1], ident32[:])
            mv = p.inc(h, "T")
            p.wait(v, "T", mv)
            h = v.tensor_reduce(red1[0:1, :], ps_red[0:1, 0:128], AX.X, ALU.max)
            fence(h, "V", v)
            h = v.tensor_copy(scrow[0:1, col:col + 1], red1[0:1, :])
            return p.inc(h, "V")

        def bcast_cc_row(cc_out_ap, tmp_row, dst_sl, ncols):
            p.wait(g, "CC")
            h = g.dma_start(out=tmp_row, in_=cc_out_ap)
            lv = p.inc(h, "qin", 16)
            p.wait(t, "qin", lv)
            h = t.matmul(ps_red[:, 0:16], ones_r32[:], tmp_row, start=True, stop=True)
            mv = p.inc(h, "T")
            p.wait(v, "T", mv)
            h = v.tensor_copy(dst_sl, ps_red[:, 0:ncols])
            val = p.inc(h, "V")
            p.wait(v, "V", val)
            return val

        h = q.dma_start(out=hfl[:], in_=hflag[:, :])
        p.inc(h, "qin", 16)
        h = q.dma_start(out=bq_sb[:], in_=b_qkv.rearrange("(a p) -> p a", p=128),
                        allow_slow_non_contiguous=True)
        p.inc(h, "qin", 16)
        h = q.dma_start(out=bg_sb[:], in_=b_gate.rearrange("(a p) -> p a", p=128),
                        allow_slow_non_contiguous=True)
        p.inc(h, "qin", 16)

        with nc.sbuf_tensor("iota_i", [128, 512], I32) as iota_i, \
             nc.sbuf_tensor("iotp_i", [128, 1], I32) as iotp_i:
            g.iota(iota_i[:], pattern=[[1, 512]], channel_multiplier=0)
            g.iota(iotp_i[:], pattern=[[0, 1]], channel_multiplier=1)
            g.memset(ones_r32[:], 1.0)
            g.memset(ones_c16[:], 1.0)
            g.memset(slots[:], 0.0)
            g.memset(scrow[:, :], 0.0)
            g.memset(epsc[:], LN_EPS)
            h = g.memset(macc[:], 0.0)
            p.inc(h, "G")
            p.wait(v, "G")
            p.wait(v, "qin")
            v.tensor_copy(iota_f[:], iota_i[:])
            h = v.tensor_copy(pvec_f[:], iotp_i[:])
            fence(h, "V", v)
            h = v.tensor_scalar(ident32[:], iota_f[:, 0:128], pvec_f[:], None, ALU.is_equal)
            fence(h, "V", v)
            v.tensor_copy(iden16[:], ident32[:])
            h = v.tensor_scalar(slot(SL_HINV), hfl[:], -1.0, 1.0, ALU.mult, ALU.add)
            p.inc(h, "V")

        st_a01 = ExitStack()
        xn2fm_hi = sbuf(st_a01, "xn2fm_hi", [128, NDK * T], BF16)
        xn2fm_lo = sbuf(st_a01, "xn2fm_lo", [128, NDK * T], BF16)

        with ExitStack() as ph:
            gb_b = sbuf(ph, "gb_b", [128, D], F32)
            bb_b = sbuf(ph, "bb_b", [128, D], F32)
            grow = sbuf(ph, "grow", [1, D], F32)
            brow = sbuf(ph, "brow", [1, D], F32)
            xt = [sbuf(ph, f"a0x{i}", [128, D], F32) for i in range(2)]
            st8 = sbuf(ph, "a0st", [128, 8], F32)
            xc = sbuf(ph, "a0xc", [128, D], F32)
            sq = sbuf(ph, "a0sq", [128, D], F32)
            xn_t2 = [sbuf(ph, f"a0xn{i}", [128, D], F32) for i in range(2)]
            xn2 = sbuf(ph, "a0xn2", [128, D], F32)
            xn2hi = sbuf(ph, "a0hi", [128, D], BF16)
            xn2lo = sbuf(ph, "a0lo", [128, D], BF16)
            red = sbuf(ph, "a0red", [128, 1], F32)
            ps_thi = psum(ph, "ps_thi", [128, 1024], BF16)
            ps_tlo = psum(ph, "ps_tlo", [128, 1024], BF16)

            h = q.dma_start(out=grow[:], in_=ln_g.unsqueeze(0))
            p.inc(h, "qin", 16)
            h = q.dma_start(out=brow[:], in_=ln_b.unsqueeze(0))
            p.inc(h, "qin", 16)
            p.wait(t, "qin")
            for row, dst in ((grow, gb_b), (brow, bb_b)):
                for hf in range(2):
                    h = t.matmul(ps_red[:, 0:512], ones_r32[:], row[:, hf * 512:(hf + 1) * 512],
                                 start=True, stop=True)
                    mv = p.inc(h, "T")
                    p.wait(v, "T", mv)
                    h = v.tensor_copy(dst[:, hf * 512:(hf + 1) * 512], ps_red[:, 0:512])
                    vv = p.inc(h, "V")
                    p.wait(t, "V", vv)

            def ln_stats(src, mu_sl, rstd_sl):
                h = v.tensor_reduce(red[:], src, AX.X, ALU.add)
                fence(h, "V", v)
                h = v.tensor_scalar(mu_sl, red[:], 1.0 / D, None, ALU.mult)
                fence(h, "V", v)
                h = v.tensor_scalar(xc[:], src, mu_sl, None, ALU.subtract)
                vv = p.inc(h, "V")
                p.wait(s, "V", vv)
                h = s.activation(sq[:], xc[:], AF.Square, bias=0.0, scale=1.0)
                sv = p.inc(h, "S")
                p.wait(v, "S", sv)
                h = v.tensor_reduce(red[:], sq[:], AX.X, ALU.add)
                vv = p.inc(h, "V")
                p.wait(s, "V", vv)
                h = s.activation(rstd_sl, red[:], AF.Sqrt, bias=epsc[:], scale=1.0 / D)
                sv = p.inc(h, "S")
                p.wait(v, "S", sv)
                h = v.reciprocal(rstd_sl, rstd_sl)
                fence(h, "V", v)

            NT0 = T // 128
            ld_done = {}
            xn_dma = {}
            for c in range(2):
                h = q.dma_start(out=xt[c][:], in_=x_in[c * 128:(c + 1) * 128, :])
                ld_done[c] = p.inc(h, "qin", 16)
            for c in range(NT0):
                buf = xt[c % 2]
                xn_t = xn_t2[c % 2]
                p.wait(v, "qin", ld_done[c])
                ln_stats(buf[:], st8[:, 0:1], st8[:, 1:2])
                if c >= 2:
                    p.wait(v, "qout", xn_dma[c - 2])
                h = v.tensor_scalar(xn_t[:], xc[:], st8[:, 1:2], None, ALU.mult)
                buf_free = p.inc(h, "V")
                p.wait(v, "V", buf_free)
                if c + 2 < NT0:
                    p.wait(q, "V", buf_free)
                    h = q.dma_start(out=xt[c % 2][:], in_=x_in[(c + 2) * 128:(c + 3) * 128, :])
                    ld_done[c + 2] = p.inc(h, "qin", 16)
                h = v.tensor_reduce(red[:], xn_t[:], AX.X, ALU.max, apply_absolute_value=True)
                fence(h, "V", v)
                h = v.tensor_tensor(macc[:, 0:1], macc[:, 0:1], red[:], ALU.max)
                vv = p.inc(h, "V")
                p.wait(g, "V", vv)
                h = g.dma_start(out=s_xn[c * 128:(c + 1) * 128, :], in_=xn_t[:])
                xn_dma[c] = p.inc(h, "qout", 16)
                if dbg:
                    h = g.dma_start(out=t_st[c * 128:(c + 1) * 128, :], in_=st8[:])
                    p.inc(h, "qout", 16)
                ln_stats(xn_t[:], st8[:, 2:3], st8[:, 3:4])
                h = v.scalar_tensor_tensor(xn2[:], xc[:], st8[:, 3:4], gb_b[:], ALU.mult, ALU.mult)
                fence(h, "V", v)
                h = v.tensor_tensor(xn2[:], xn2[:], bb_b[:], ALU.add)
                fence(h, "V", v)
                h = split2(xn2hi[:], xn2lo[:], xn2[:])
                vv = p.inc(h, "V")
                p.wait(t, "V", vv)
                for dc in range(NDK):
                    t.transpose(ps_thi[:, dc * 128:(dc + 1) * 128], xn2hi[:, dc * 128:(dc + 1) * 128],
                                iden16[:])
                    h = t.transpose(ps_tlo[:, dc * 128:(dc + 1) * 128], xn2lo[:, dc * 128:(dc + 1) * 128],
                                    iden16[:])
                mv = p.inc(h, "T")
                p.wait(v, "T", mv)
                for dc in range(NDK):
                    v.tensor_copy(xn2fm_hi[:, dc * T + c * 128: dc * T + (c + 1) * 128],
                                  ps_thi[:, dc * 128:(dc + 1) * 128])
                    h = v.tensor_copy(xn2fm_lo[:, dc * T + c * 128: dc * T + (c + 1) * 128],
                                      ps_tlo[:, dc * 128:(dc + 1) * 128])
                vv = p.inc(h, "V")
                p.wait(t, "V", vv)

            vv = part_max_to_col(0, 0)
            p.wait(g, "V", vv)
            h = g.dma_start(out=cc1_in[:, :], in_=scrow[0:1, 0:16])
            p.inc(h, "qout", 16)
            p.wait(g, "qout")
            h = g.collective_compute("AllReduce", ALU.max, replica_groups=GRP_ALL,
                                     ins=[cc1_in.opt()], outs=[cc1_out.opt()])
            p.inc(h, "CC")
            phase_barrier()

        p.wait(q, "CC")
        with nc.sbuf_tensor("m1row", [1, 16], F32) as m1row:
            bcast_cc_row(cc1_out[:, :], m1row[:], slot(SL_S1), 1)
            if dbg:
                h = v.tensor_copy(dbg_sb[0:1, 0:16], m1row[:])
                p.inc(h, "V")
            h = v.tensor_scalar(slot(SL_S1), slot(SL_S1), 1.0 / 32767.0, 1e-12, ALU.mult, ALU.max)
            fence(h, "V", v)
            h = v.reciprocal(slot(SL_RS1), slot(SL_S1))
            fence(h, "V", v)
            phase_barrier()

        if dbg:
            h = v.tensor_copy(dbg_sb[:, 32:48], slots[:])
            h = v.tensor_copy(dbg_sb[0:2, 16:32], scrow[:, :])
            vv = p.inc(h, "V")
            p.wait(g, "V", vv)
            h = g.dma_start(out=t_dbg[:, :], in_=dbg_sb[:])
            p.inc(h, "qout", 16)

        with ExitStack() as ph:
            wtmp = [sbuf(ph, f"a1w{i}", [128, NDK * 128], F32) for i in range(2)]
            whi = [sbuf(ph, f"a1whi{i}", [128, NDK * 128], BF16) for i in range(2)]
            wlo = [sbuf(ph, f"a1wlo{i}", [128, NDK * 128], BF16) for i in range(2)]
            gate_buf = sbuf(ph, "a1gate", [128, T], F32)
            ktmp = sbuf(ph, "a1ktmp", [128, 512], F32)
            ktmp2 = sbuf(ph, "a1ktmp2", [128, 512], F32)
            etile = sbuf(ph, "a1e", [128, 512], F32)
            kptile = sbuf(ph, "a1kp", [128, 512], F32)
            obhi = [sbuf(ph, f"a1oh{i}", [128, T], BF16) for i in range(2)]
            oblo = [sbuf(ph, f"a1ol{i}", [128, T], BF16) for i in range(2)]
            ps_a1 = [psum(ph, f"ps_a1{i}", [128, 512], F32) for i in range(2)]

            sched = []
            for j in range(NDK):
                sched.append(("gate", j))
                sched.append(("k", j))
            for j in range(NDK):
                sched.append(("q", j))
            for j in range(NDK):
                sched.append(("v", j))

            def wsrc(role, j):
                if role == "gate":
                    return w_gate[:, j * 128:(j + 1) * 128]
                off = {"q": 0, "k": D, "v": 2 * D}[role] + j * 128
                return w_qkv[:, off:off + 128]

            def bias_col(role, j):
                if role == "gate":
                    return bg_sb[:, j:j + 1]
                a = {"q": 0, "k": 8, "v": 16}[role] + j
                return bq_sb[:, a:a + 1]

            ostore = {"k": (s_khi, s_klo), "q": (s_qhi, s_qlo), "v": (s_vhi, s_vlo)}

            ld_w = {}
            split_done = {}
            mm_done = {}
            evac_done = []
            ob_dma = {}
            gate_read_done = {}
            n_uses = 0

            for c in range(2):
                h = q.dma_start(out=wtmp[c][:].rearrange("p (a n) -> p a n", a=NDK),
                                in_=wsrc(*sched[c]).rearrange("(a p) n -> p a n", p=128))
                ld_w[c] = p.inc(h, "qin", 16)

            for c, (role, j) in enumerate(sched):
                par = c % 2
                p.wait(v, "qin", ld_w[c])
                if c >= 2:
                    p.wait(v, "T", mm_done[c - 2])
                h = split2(whi[par][:], wlo[par][:], wtmp[par][:])
                split_done[c] = p.inc(h, "V")
                if c + 2 < len(sched):
                    p.wait(q, "V", split_done[c])
                    h = q.dma_start(out=wtmp[par][:].rearrange("p (a n) -> p a n", a=NDK),
                                    in_=wsrc(*sched[c + 2]).rearrange("(a p) n -> p a n", p=128))
                    ld_w[c + 2] = p.inc(h, "qin", 16)
                if role != "gate" and (c - 2) in ob_dma:
                    p.wait(v, "qout", ob_dma[c - 2])
                p.wait(t, "V", split_done[c])
                for nt in range(8):
                    pp = n_uses % 2
                    if n_uses >= 2:
                        sem, val = evac_done[n_uses - 2]
                        p.wait(t, sem, val)
                    for kc in range(NDK):
                        h = mm3(ps_a1[pp][:],
                                whi[par][:, kc * 128:(kc + 1) * 128],
                                wlo[par][:, kc * 128:(kc + 1) * 128],
                                xn2fm_hi[:, kc * T + nt * 512: kc * T + (nt + 1) * 512],
                                xn2fm_lo[:, kc * T + nt * 512: kc * T + (nt + 1) * 512],
                                start=(kc == 0), stop=(kc == NDK - 1))
                    mmv = p.inc(h, "T")
                    if nt == 7:
                        mm_done[c] = mmv
                    sl = slice(nt * 512, (nt + 1) * 512)
                    if role == "gate":
                        p.wait(s, "T", mmv)
                        if j in gate_read_done and nt == 0:
                            pass
                        h = s.activation(gate_buf[:, sl], ps_a1[pp][:], AF.Sigmoid,
                                         bias=bias_col(role, j), scale=1.0)
                        evac_done.append(("S", p.inc(h, "S")))
                    elif role == "k":
                        p.wait(v, "T", mmv)
                        gsem, gval = evac_done[(c - 1) * 8 + nt]
                        p.wait(v, gsem, gval)
                        h = v.scalar_tensor_tensor(ktmp[:], ps_a1[pp][:], bias_col(role, j),
                                                   gate_buf[:, sl], ALU.add, ALU.mult)
                        fence(h, "V", v)
                        h = v.tensor_scalar(ktmp2[:], ktmp[:], 0.0, None, ALU.min)
                        vv = p.inc(h, "V")
                        p.wait(s, "V", vv)
                        h = s.activation(etile[:], ktmp2[:], AF.Exp, bias=0.0, scale=1.0)
                        sv = p.inc(h, "S")
                        p.wait(v, "S", sv)
                        h = v.scalar_tensor_tensor(kptile[:], ktmp[:], 0.0, etile[:],
                                                   ALU.max, ALU.add)
                        fence(h, "V", v)
                        h = split2(obhi[par][:, sl], oblo[par][:, sl], kptile[:])
                        evac_done.append(("V", p.inc(h, "V")))
                    elif role == "q":
                        p.wait(v, "T", mmv)
                        h = v.tensor_scalar(ktmp2[:], ps_a1[pp][:], bias_col(role, j), 0.0,
                                            ALU.add, ALU.min)
                        vv = p.inc(h, "V")
                        p.wait(s, "V", vv)
                        h = s.activation(etile[:], ktmp2[:], AF.Exp, bias=0.0, scale=1.0)
                        sv = p.inc(h, "S")
                        h = v.tensor_scalar(ktmp[:], ps_a1[pp][:], bias_col(role, j), 0.0,
                                            ALU.add, ALU.max)
                        fence(h, "V", v)
                        p.wait(v, "S", sv)
                        h = v.tensor_tensor(kptile[:], ktmp[:], etile[:], ALU.add)
                        fence(h, "V", v)
                        h = split2(obhi[par][:, sl], oblo[par][:, sl], kptile[:])
                        evac_done.append(("V", p.inc(h, "V")))
                    else:
                        p.wait(v, "T", mmv)
                        h = v.tensor_scalar(ktmp[:], ps_a1[pp][:], bias_col(role, j), None,
                                            ALU.add)
                        fence(h, "V", v)
                        h = split2(obhi[par][:, sl], oblo[par][:, sl], ktmp[:])
                        evac_done.append(("V", p.inc(h, "V")))
                    n_uses += 1
                if role != "gate":
                    shi, slo = ostore[role]
                    sem, val = evac_done[-1]
                    p.wait(g, sem, val)
                    h = g.dma_start(out=shi[j * 128:(j + 1) * 128, :], in_=obhi[par][:])
                    p.inc(h, "qout", 16)
                    h = g.dma_start(out=slo[j * 128:(j + 1) * 128, :], in_=oblo[par][:])
                    ob_dma[c] = p.inc(h, "qout", 16)
                else:
                    gate_read_done[j] = None
            phase_barrier()
        st_a01.close()

        SW = 1032
        with ExitStack() as ph:
            S_sb = sbuf(ph, "S_sb", [128, NDK * SW], F32)
            Shi = sbuf(ph, "Shi", [128, NDK * SW], BF16)
            Slo = sbuf(ph, "Slo", [128, NDK * SW], BF16)
            masks = sbuf(ph, "a2mask", [128, 4 * 512], F32)
            qfhi = sbuf(ph, "a2qfh", [128, NDK * 512], BF16)
            qflo = sbuf(ph, "a2qfl", [128, NDK * 512], BF16)
            kfhi = sbuf(ph, "a2kfh", [128, NDK * 512], BF16)
            kflo = sbuf(ph, "a2kfl", [128, NDK * 512], BF16)
            kthi = sbuf(ph, "a2kth", [128, 4 * 1024], BF16)
            ktlo = sbuf(ph, "a2ktl", [128, 4 * 1024], BF16)
            vthi = sbuf(ph, "a2vth", [128, 4 * 1024], BF16)
            vtlo = sbuf(ph, "a2vtl", [128, 4 * 1024], BF16)
            amhi = sbuf(ph, "a2amh", [128, 4 * 512], BF16)
            amlo = sbuf(ph, "a2aml", [128, 4 * 512], BF16)
            am32 = sbuf(ph, "a2am32", [128, 512], F32)
            numst = sbuf(ph, "a2num", [128, NDK * 512], F32)
            denst = sbuf(ph, "a2den", [1, 512], F32)
            psA = psum(ph, "psA", [128, 512], F32)
            psN = [psum(ph, f"psN{i}", [128, 512], F32) for i in range(2)]
            psS = [psum(ph, f"psS{i}", [128, 512], F32) for i in range(2)]
            psD = psum(ph, "psD", [1, 512], F32)
            psK = psum(ph, "psK", [128, 1], F32)

            for si in range(4):
                h = v.tensor_scalar(masks[:, si * 512:(si + 1) * 512], iota_f[:], pvec_f[:],
                                    float(1 - si * 128), ALU.subtract, ALU.add)
                fence(h, "V", v)
                h = v.tensor_scalar(masks[:, si * 512:(si + 1) * 512],
                                    masks[:, si * 512:(si + 1) * 512], 0.0, 1.0, ALU.max, ALU.min)
                fence(h, "V", v)

            state_T_done = 0
            s_split_done = 0
            num_dma = 0
            psn_evac = []
            pss_evac = []
            psd_evac = 0
            psk_evac = 0
            psa_evac = 0
            n_psn = 0
            n_pss = 0

            for c in range(NCA):
                base = c * CA
                if c > 0:
                    p.wait(q, "T", state_T_done)
                h = q.dma_start(out=qfhi[:].rearrange("p (a n) -> p a n", a=NDK),
                                in_=s_qhi[:, base:base + CA].rearrange("(a p) n -> p a n", p=128))
                p.inc(h, "qin", 16)
                h = q.dma_start(out=qflo[:].rearrange("p (a n) -> p a n", a=NDK),
                                in_=s_qlo[:, base:base + CA].rearrange("(a p) n -> p a n", p=128))
                p.inc(h, "qin", 16)
                h = q.dma_start(out=kfhi[:].rearrange("p (a n) -> p a n", a=NDK),
                                in_=s_khi[:, base:base + CA].rearrange("(a p) n -> p a n", p=128))
                p.inc(h, "qin", 16)
                h = q.dma_start(out=kflo[:].rearrange("p (a n) -> p a n", a=NDK),
                                in_=s_klo[:, base:base + CA].rearrange("(a p) n -> p a n", p=128))
                p.inc(h, "qin", 16)
                for si in range(4):
                    cs = slice(base + si * 128, base + (si + 1) * 128)
                    h = q.dma_start(out=kthi[:, si * 1024:(si + 1) * 1024], in_=s_khi[:, cs],
                                    transpose=True)
                    p.inc(h, "qin", 16)
                    h = q.dma_start(out=ktlo[:, si * 1024:(si + 1) * 1024], in_=s_klo[:, cs],
                                    transpose=True)
                    p.inc(h, "qin", 16)
                    h = q.dma_start(out=vthi[:, si * 1024:(si + 1) * 1024], in_=s_vhi[:, cs],
                                    transpose=True)
                    p.inc(h, "qin", 16)
                    h = q.dma_start(out=vtlo[:, si * 1024:(si + 1) * 1024], in_=s_vlo[:, cs],
                                    transpose=True)
                    p.inc(h, "qin", 16)
                lds = p.val["qin"]

                p.wait(t, "qin", lds)
                for si in range(4):
                    if si > 0 or c > 0:
                        p.wait(t, "V", psa_evac)
                    for kc in range(NDK):
                        h = mm3(psA[:],
                                kfhi[:, kc * 512 + si * 128: kc * 512 + (si + 1) * 128],
                                kflo[:, kc * 512 + si * 128: kc * 512 + (si + 1) * 128],
                                qfhi[:, kc * 512:(kc + 1) * 512],
                                qflo[:, kc * 512:(kc + 1) * 512],
                                start=(kc == 0), stop=(kc == NDK - 1))
                    mmv = p.inc(h, "T")
                    p.wait(v, "T", mmv)
                    h = v.tensor_tensor(am32[:], psA[:], masks[:, si * 512:(si + 1) * 512],
                                        ALU.mult)
                    fence(h, "V", v)
                    h = split2(amhi[:, si * 512:(si + 1) * 512],
                               amlo[:, si * 512:(si + 1) * 512], am32[:])
                    psa_evac = p.inc(h, "V")

                if c > 0:
                    p.wait(t, "V", s_split_done)
                if c > 0:
                    p.wait(v, "qout", num_dma)
                for dvc in range(NDK):
                    pp = n_psn % 2
                    if n_psn >= 2:
                        p.wait(t, "V", psn_evac[n_psn - 2])
                    first = True
                    if c > 0:
                        for kc in range(NDK):
                            h = mm3(psN[pp][:],
                                    Shi[:, kc * SW + dvc * 128: kc * SW + (dvc + 1) * 128],
                                    Slo[:, kc * SW + dvc * 128: kc * SW + (dvc + 1) * 128],
                                    qfhi[:, kc * 512:(kc + 1) * 512],
                                    qflo[:, kc * 512:(kc + 1) * 512],
                                    start=(kc == 0), stop=False)
                        first = False
                    for si in range(4):
                        h = mm3(psN[pp][:],
                                vthi[:, si * 1024 + dvc * 128: si * 1024 + (dvc + 1) * 128],
                                vtlo[:, si * 1024 + dvc * 128: si * 1024 + (dvc + 1) * 128],
                                amhi[:, si * 512:(si + 1) * 512],
                                amlo[:, si * 512:(si + 1) * 512],
                                start=(first and si == 0), stop=(si == 3))
                    mmv = p.inc(h, "T")
                    p.wait(v, "T", mmv)
                    h = v.tensor_copy(numst[:, dvc * 512:(dvc + 1) * 512], psN[pp][:])
                    psn_evac.append(p.inc(h, "V"))
                    n_psn += 1
                if psd_evac:
                    p.wait(t, "V", psd_evac)
                first = True
                if c > 0:
                    for kc in range(NDK):
                        h = mm3(psD[:],
                                Shi[:, kc * SW + 1024: kc * SW + 1025],
                                Slo[:, kc * SW + 1024: kc * SW + 1025],
                                qfhi[:, kc * 512:(kc + 1) * 512],
                                qflo[:, kc * 512:(kc + 1) * 512],
                                start=(kc == 0), stop=False)
                    first = False
                for si in range(4):
                    t.matmul(psD[:], ones_c16[:], amhi[:, si * 512:(si + 1) * 512],
                             start=(first and si == 0), stop=False)
                    h = t.matmul(psD[:], ones_c16[:], amlo[:, si * 512:(si + 1) * 512],
                                 start=False, stop=(si == 3))
                mmv = p.inc(h, "T")
                p.wait(v, "T", mmv)
                h = v.tensor_copy(denst[:], psD[:])
                psd_evac = p.inc(h, "V")

                p.wait(g, "V", psn_evac[-1])
                p.wait(g, "V", psd_evac)
                h = g.dma_start(out=s_num[:, base:base + CA].rearrange("(a p) n -> p a n", p=128),
                                in_=numst[:].rearrange("p (a n) -> p a n", a=NDK))
                p.inc(h, "qout", 16)
                h = g.dma_start(out=s_den[0:1, base:base + CA], in_=denst[:])
                num_dma = p.inc(h, "qout", 16)

                for kc in range(NDK):
                    for nt2 in range(2):
                        pp = n_pss % 2
                        if n_pss >= 2:
                            p.wait(t, "V", pss_evac[n_pss - 2])
                        for si in range(4):
                            h = mm3(psS[pp][:],
                                    kthi[:, si * 1024 + kc * 128: si * 1024 + (kc + 1) * 128],
                                    ktlo[:, si * 1024 + kc * 128: si * 1024 + (kc + 1) * 128],
                                    vthi[:, si * 1024 + nt2 * 512: si * 1024 + (nt2 + 1) * 512],
                                    vtlo[:, si * 1024 + nt2 * 512: si * 1024 + (nt2 + 1) * 512],
                                    start=(si == 0), stop=(si == 3))
                        mmv = p.inc(h, "T")
                        p.wait(v, "T", mmv)
                        dst = S_sb[:, kc * SW + nt2 * 512: kc * SW + (nt2 + 1) * 512]
                        if c == 0:
                            h = v.tensor_copy(dst, psS[pp][:])
                        else:
                            h = v.tensor_tensor(dst, dst, psS[pp][:], ALU.add)
                        pss_evac.append(p.inc(h, "V"))
                        n_pss += 1
                    if psk_evac:
                        p.wait(t, "V", psk_evac)
                    for si in range(4):
                        t.matmul(psK[:], kthi[:, si * 1024 + kc * 128: si * 1024 + (kc + 1) * 128],
                                 ones_c16[:], start=(si == 0), stop=False)
                        h = t.matmul(psK[:], ktlo[:, si * 1024 + kc * 128: si * 1024 + (kc + 1) * 128],
                                     ones_c16[:], start=False, stop=(si == 3))
                    mmv = p.inc(h, "T")
                    state_T_done = mmv
                    p.wait(v, "T", mmv)
                    dstk = S_sb[:, kc * SW + 1024: kc * SW + 1025]
                    if c == 0:
                        h = v.tensor_copy(dstk, psK[:])
                    else:
                        h = v.tensor_tensor(dstk, dstk, psK[:], ALU.add)
                    psk_evac = p.inc(h, "V")

                h = split2(Shi[:], Slo[:], S_sb[:])
                s_split_done = p.inc(h, "V")

            if dbg:
                p.wait(g, "V", s_split_done)
                h = g.dma_start(out=t_S[:, :], in_=S_sb[:])
                p.inc(h, "qout", 16)

            p.wait(v, "qout")
            h = v.tensor_scalar(S_sb[:], S_sb[:], slot(SL_HINV), None, ALU.mult)
            vv = p.inc(h, "V")
            p.wait(g, "V", vv)
            h = g.dma_start(out=cc_s_in[:, :], in_=S_sb[:])
            p.inc(h, "qout", 16)
            p.wait(g, "qout")
            h = g.collective_compute("AllReduce", ALU.add, replica_groups=GRP_PAIR,
                                     ins=[cc_s_in.opt()], outs=[cc_s_out.opt()])
            p.inc(h, "CC")
            phase_barrier()

        st_ffn = ExitStack()
        xqfm_hi = sbuf(st_ffn, "xqfm_hi", [128, NDK * T], BF16)
        xqfm_lo = sbuf(st_ffn, "xqfm_lo", [128, NDK * T], BF16)
        with ExitStack() as ph:
            xt = [sbuf(ph, f"xq_x{i}", [128, D], F32) for i in range(2)]
            xqt = sbuf(ph, "xq_t", [128, D], F32)
            xqhi_t = sbuf(ph, "xq_hi", [128, D], BF16)
            xqlo_t = sbuf(ph, "xq_lo", [128, D], BF16)
            ps_thi = psum(ph, "ps_xqh", [128, 1024], BF16)
            ps_tlo = psum(ph, "ps_xql", [128, 1024], BF16)
            NT0 = T // 128
            ld_done = {}
            for c in range(2):
                h = q.dma_start(out=xt[c][:], in_=s_xn[c * 128:(c + 1) * 128, :])
                ld_done[c] = p.inc(h, "qin", 16)
            for c in range(NT0):
                buf = xt[c % 2]
                p.wait(v, "qin", ld_done[c])
                h = v.tensor_scalar(xqt[:], buf[:], slot(SL_RS1), MAGIC, ALU.mult, ALU.add)
                buf_free = p.inc(h, "V")
                p.wait(v, "V", buf_free)
                if c + 2 < NT0:
                    p.wait(q, "V", buf_free)
                    h = q.dma_start(out=xt[c % 2][:], in_=s_xn[(c + 2) * 128:(c + 3) * 128, :])
                    ld_done[c + 2] = p.inc(h, "qin", 16)
                h = v.tensor_scalar(xqt[:], xqt[:], MAGIC, -32768.0, ALU.subtract, ALU.max)
                fence(h, "V", v)
                h = v.tensor_scalar(xqt[:], xqt[:], 32767.0, None, ALU.min)
                fence(h, "V", v)
                h = split2(xqhi_t[:], xqlo_t[:], xqt[:])
                vv = p.inc(h, "V")
                p.wait(t, "V", vv)
                for dc in range(NDK):
                    t.transpose(ps_thi[:, dc * 128:(dc + 1) * 128], xqhi_t[:, dc * 128:(dc + 1) * 128],
                                iden16[:])
                    h = t.transpose(ps_tlo[:, dc * 128:(dc + 1) * 128], xqlo_t[:, dc * 128:(dc + 1) * 128],
                                    iden16[:])
                mv = p.inc(h, "T")
                p.wait(v, "T", mv)
                for dc in range(NDK):
                    v.tensor_copy(xqfm_hi[:, dc * T + c * 128: dc * T + (c + 1) * 128],
                                  ps_thi[:, dc * 128:(dc + 1) * 128])
                    h = v.tensor_copy(xqfm_lo[:, dc * T + c * 128: dc * T + (c + 1) * 128],
                                      ps_tlo[:, dc * 128:(dc + 1) * 128])
                vv = p.inc(h, "V")
                p.wait(t, "V", vv)
            phase_barrier()

        with ExitStack() as ph:
            wtmp = [sbuf(ph, f"f2w{i}", [128, 3 * D], F32) for i in range(2)]
            blkm = sbuf(ph, "f2blk", [128, 96], F32)
            sblk = sbuf(ph, "f2s", [128, 96], F32)
            s1s = sbuf(ph, "f2s1s", [128, 96], F32)
            rqb = sbuf(ph, "f2rq", [128, 96], F32)
            t2 = sbuf(ph, "f2t2", [128, 3 * D], F32)
            wdhi = sbuf(ph, "f2wh", [128, 3 * D], BF16)
            wdlo = sbuf(ph, "f2wl", [128, 3 * D], BF16)
            wThi = sbuf(ph, "f2wth", [128, NDK * 128], BF16)
            wTlo = sbuf(ph, "f2wtl", [128, NDK * 128], BF16)
            stg = [sbuf(ph, f"f2stg{i}", [128, 512], F32) for i in range(2)]
            red2 = sbuf(ph, "f2red", [128, 1], F32)
            ps_tw = [psum(ph, f"ps_tw{i}", [128, 1024], BF16) for i in range(2)]
            ps_f = [psum(ph, f"ps_f{i}", [128, 512], F32) for i in range(2)]

            sched = []
            for j in range(DI // 128):
                sched.append(("up", j))
                sched.append(("gate", j))
                if j < NDK:
                    sched.append(("wdown", j))

            def src_ap(role, j):
                if role == "up":
                    return fw_up[j * 128:(j + 1) * 128, :]
                if role == "gate":
                    return fw_gate[j * 128:(j + 1) * 128, :]
                return fw_down[j * 128:(j + 1) * 128, :]

            ld_w = {}
            wfree = {}
            stg_dma = {}
            maccol = {"up": 1, "gate": 2}
            n_uses = 0
            evac_done = []

            for c in range(2):
                role, j = sched[c]
                W = D if role != "wdown" else 3 * D
                h = q.dma_start(out=wtmp[c][:, 0:W], in_=src_ap(role, j))
                ld_w[c] = p.inc(h, "qin", 16)

            for c, (role, j) in enumerate(sched):
                par = c % 2
                W = D if role != "wdown" else 3 * D
                NB = W // 32
                p.wait(v, "qin", ld_w[c])
                h = v.tensor_reduce(blkm[:, 0:NB], wtmp[par][:, 0:W].rearrange("p (g b) -> p g b", b=32),
                                    AX.X, ALU.max, apply_absolute_value=True)
                fence(h, "V", v)
                h = v.tensor_scalar(sblk[:, 0:NB], blkm[:, 0:NB], 1.0 / 7.0, 1e-12, ALU.mult, ALU.max)
                fence(h, "V", v)
                if role != "wdown":
                    h = v.tensor_scalar(s1s[:, 0:NB], sblk[:, 0:NB], slot(SL_S1), None, ALU.mult)
                    fence(h, "V", v)
                h = v.reciprocal(rqb[:, 0:NB], sblk[:, 0:NB])
                fence(h, "V", v)
                h = v.tensor_tensor(t2[:, 0:W].rearrange("p (g b) -> p g b", b=32),
                                    wtmp[par][:, 0:W].rearrange("p (g b) -> p g b", b=32),
                                    rqb[:, 0:NB].unsqueeze(-1).broadcast_to([128, NB, 32]), ALU.mult)
                wfree[c] = p.inc(h, "V")
                p.wait(v, "V", wfree[c])
                if c + 2 < len(sched):
                    role2, j2 = sched[c + 2]
                    W2 = D if role2 != "wdown" else 3 * D
                    p.wait(q, "V", wfree[c])
                    h = q.dma_start(out=wtmp[par][:, 0:W2], in_=src_ap(role2, j2))
                    ld_w[c + 2] = p.inc(h, "qin", 16)
                h = v.tensor_scalar(t2[:, 0:W], t2[:, 0:W], MAGIC, MAGIC, ALU.add, ALU.subtract)
                fence(h, "V", v)
                h = v.tensor_scalar(t2[:, 0:W], t2[:, 0:W], -8.0, 7.0, ALU.max, ALU.min)
                fence(h, "V", v)
                sc = s1s if role != "wdown" else sblk
                h = v.tensor_tensor(t2[:, 0:W].rearrange("p (g b) -> p g b", b=32),
                                    t2[:, 0:W].rearrange("p (g b) -> p g b", b=32),
                                    sc[:, 0:NB].unsqueeze(-1).broadcast_to([128, NB, 32]), ALU.mult)
                fence(h, "V", v)
                h = split2(wdhi[:, 0:W], wdlo[:, 0:W], t2[:, 0:W])
                spl = p.inc(h, "V")

                if role == "wdown":
                    p.wait(g, "V", spl)
                    h = g.dma_start(out=s_wdthi[j * 128:(j + 1) * 128, :], in_=wdhi[:, 0:W])
                    p.inc(h, "qout", 16)
                    h = g.dma_start(out=s_wdtlo[j * 128:(j + 1) * 128, :], in_=wdlo[:, 0:W])
                    p.inc(h, "qout", 16)
                    continue

                p.wait(t, "V", spl)
                for hc in range(NDK):
                    t.transpose(ps_tw[0][:, hc * 128:(hc + 1) * 128],
                                wdhi[:, hc * 128:(hc + 1) * 128], iden16[:])
                    h = t.transpose(ps_tw[1][:, hc * 128:(hc + 1) * 128],
                                    wdlo[:, hc * 128:(hc + 1) * 128], iden16[:])
                mv = p.inc(h, "T")
                p.wait(v, "T", mv)
                for hc in range(NDK):
                    v.tensor_copy(wThi[:, hc * 128:(hc + 1) * 128], ps_tw[0][:, hc * 128:(hc + 1) * 128])
                    h = v.tensor_copy(wTlo[:, hc * 128:(hc + 1) * 128], ps_tw[1][:, hc * 128:(hc + 1) * 128])
                wt_done = p.inc(h, "V")
                p.wait(t, "V", wt_done)

                dst = s_upraw if role == "up" else s_gateraw
                for nt in range(8):
                    pp = n_uses % 2
                    if n_uses >= 2:
                        sem, val = evac_done[n_uses - 2]
                        p.wait(t, sem, val)
                    for hc in range(NDK):
                        h = mm3(ps_f[pp][:],
                                wThi[:, hc * 128:(hc + 1) * 128],
                                wTlo[:, hc * 128:(hc + 1) * 128],
                                xqfm_hi[:, hc * T + nt * 512: hc * T + (nt + 1) * 512],
                                xqfm_lo[:, hc * T + nt * 512: hc * T + (nt + 1) * 512],
                                start=(hc == 0), stop=(hc == NDK - 1))
                    mmv = p.inc(h, "T")
                    p.wait(v, "T", mmv)
                    key = n_uses - 2
                    if key in stg_dma:
                        p.wait(v, "qout", stg_dma[key])
                    h = v.tensor_copy(stg[pp][:], ps_f[pp][:])
                    fence(h, "V", v)
                    h = v.tensor_reduce(red2[:], stg[pp][:], AX.X, ALU.max, apply_absolute_value=True)
                    fence(h, "V", v)
                    mc = maccol[role]
                    h = v.tensor_tensor(macc[:, mc:mc + 1], macc[:, mc:mc + 1], red2[:], ALU.max)
                    evac_done.append(("V", p.inc(h, "V")))
                    p.wait(g, "V", evac_done[-1][1])
                    h = g.dma_start(out=dst[j * 128:(j + 1) * 128, nt * 512:(nt + 1) * 512],
                                    in_=stg[pp][:])
                    stg_dma[n_uses] = p.inc(h, "qout", 16)
                    n_uses += 1

            vv = part_max_to_col(0, 1)
            vv = part_max_to_col(1, 2)
            p.wait(g, "V", vv)
            h = g.dma_start(out=cc23_in[:, :], in_=scrow[0:1, 0:16])
            p.inc(h, "qout", 16)
            p.wait(g, "qout")
            h = g.collective_compute("AllReduce", ALU.max, replica_groups=GRP_ALL,
                                     ins=[cc23_in.opt()], outs=[cc23_out.opt()])
            p.inc(h, "CC")
            phase_barrier()
        st_ffn.close()

        p.wait(q, "CC")
        with nc.sbuf_tensor("m23row", [1, 16], F32) as mrow:
            p.wait(g, "CC")
            h = g.dma_start(out=mrow[:], in_=cc23_out[:, :])
            lv = p.inc(h, "qin", 16)
            p.wait(t, "qin", lv)
            h = t.matmul(ps_red[:, 0:16], ones_r32[:], mrow[:], start=True, stop=True)
            mv = p.inc(h, "T")
            p.wait(v, "T", mv)
            v.tensor_copy(slot(SL_S2), ps_red[:, 0:1])
            h = v.tensor_copy(slot(SL_S3), ps_red[:, 1:2])
            fence(h, "V", v)
            h = v.tensor_scalar(slot(SL_S2), slot(SL_S2), 1.0 / 32767.0, 1e-12, ALU.mult, ALU.max)
            fence(h, "V", v)
            h = v.reciprocal(slot(SL_RS2), slot(SL_S2))
            fence(h, "V", v)
            h = v.tensor_scalar(slot(SL_S3), slot(SL_S3), 1.0 / 32767.0, 1e-12, ALU.mult, ALU.max)
            fence(h, "V", v)
            h = v.reciprocal(slot(SL_RS3), slot(SL_S3))
            fence(h, "V", v)
            phase_barrier()

        if dbg:
            h = v.tensor_copy(dbg_sb[:, 48:64], slots[:])
            vv = p.inc(h, "V")
            p.wait(g, "V", vv)
            h = g.dma_start(out=t_dbg[:, 48:64], in_=dbg_sb[:, 48:64])
            p.inc(h, "qout", 16)

        SW = 1032
        with ExitStack() as ph:
            SRhi = sbuf(ph, "SRhi", [128, NDK * SW], BF16)
            SRlo = sbuf(ph, "SRlo", [128, NDK * SW], BF16)
            wphi = sbuf(ph, "wphi", [128, NDK * D], BF16)
            wplo = sbuf(ph, "wplo", [128, NDK * D], BF16)
            bpb = sbuf(ph, "bpb", [128, D], F32)
            with ExitStack() as ph2:
                SR = sbuf(ph2, "SR", [128, NDK * SW], F32)
                wp32 = sbuf(ph2, "wp32", [128, D], F32)
                bprow = sbuf(ph2, "bprow", [1, D], F32)
                p.wait(q, "CC")
                h = q.dma_start(out=SR[:], in_=cc_s_out[:, :])
                lv = p.inc(h, "qin", 16)
                p.wait(v, "qin", lv)
                h = v.tensor_scalar(SR[:], SR[:], hfl[:], None, ALU.mult)
                fence(h, "V", v)
                h = split2(SRhi[:], SRlo[:], SR[:])
                p.inc(h, "V")
                for kc in range(NDK):
                    h = q.dma_start(out=wp32[:], in_=w_proj[kc * 128:(kc + 1) * 128, :])
                    lv = p.inc(h, "qin", 16)
                    p.wait(v, "qin", lv)
                    h = split2(wphi[:, kc * D:(kc + 1) * D], wplo[:, kc * D:(kc + 1) * D], wp32[:])
                    vv = p.inc(h, "V")
                    if kc + 1 < NDK:
                        p.wait(q, "V", vv)
                h = q.dma_start(out=bprow[:], in_=b_proj.unsqueeze(0))
                lv = p.inc(h, "qin", 16)
                p.wait(t, "qin", lv)
                for hf in range(2):
                    h = t.matmul(ps_red[:, 0:512], ones_r32[:], bprow[:, hf * 512:(hf + 1) * 512],
                                 start=True, stop=True)
                    mv = p.inc(h, "T")
                    p.wait(v, "T", mv)
                    h = v.tensor_copy(bpb[:, hf * 512:(hf + 1) * 512], ps_red[:, 0:512])
                    vv = p.inc(h, "V")
                    p.wait(t, "V", vv)
                phase_barrier()

            with ExitStack() as ph3:
                numst = sbuf(ph3, "fnum", [128, NDK * 512], F32)
                denrow = sbuf(ph3, "fden", [1, 512], F32)
                qfhi = sbuf(ph3, "fqfh", [128, NDK * 512], BF16)
                qflo = sbuf(ph3, "fqfl", [128, NDK * 512], BF16)
                xrt = sbuf(ph3, "fxrt", [128, 4 * D], F32)
                ndt = sbuf(ph3, "fndt", [128, 512], F32)
                ndhi = sbuf(ph3, "fndh", [128, NDK * 512], BF16)
                ndlo = sbuf(ph3, "fndl", [128, NDK * 512], BF16)
                rstg = sbuf(ph3, "frst", [128, D], F32)
                psC = [psum(ph3, f"psC{i}", [128, 512], F32) for i in range(2)]
                psB = psum(ph3, "psB", [128, 512], F32)
                psD2 = psum(ph3, "psD2", [1, 512], F32)

                ld_prev_T = 0
                ld_prev_V = 0
                rstg_dma = 0
                for cf in range(NCA):
                    base = cf * CA
                    if cf > 0:
                        p.wait(q, "T", ld_prev_T)
                        p.wait(q, "V", ld_prev_V)
                    h = q.dma_start(out=numst[:].rearrange("p (a n) -> p a n", a=NDK),
                                    in_=s_num[:, base:base + CA].rearrange("(a p) n -> p a n", p=128))
                    p.inc(h, "qin", 16)
                    h = q.dma_start(out=denrow[:], in_=s_den[0:1, base:base + CA])
                    p.inc(h, "qin", 16)
                    h = q.dma_start(out=qfhi[:].rearrange("p (a n) -> p a n", a=NDK),
                                    in_=s_qhi[:, base:base + CA].rearrange("(a p) n -> p a n", p=128))
                    p.inc(h, "qin", 16)
                    h = q.dma_start(out=qflo[:].rearrange("p (a n) -> p a n", a=NDK),
                                    in_=s_qlo[:, base:base + CA].rearrange("(a p) n -> p a n", p=128))
                    p.inc(h, "qin", 16)
                    h = q.dma_start(out=xrt[:].rearrange("p (a n) -> p a n", a=4),
                                    in_=x_in[base:base + CA, :].rearrange("(a p) n -> p a n", p=128))
                    lds = p.inc(h, "qin", 16)
                    p.wait(t, "qin", lds)
                    psc_evac = []
                    for dvc in range(NDK):
                        pp = dvc % 2
                        if dvc >= 2:
                            p.wait(t, "V", psc_evac[dvc - 2])
                        for kc in range(NDK):
                            h = mm3(psC[pp][:],
                                    SRhi[:, kc * SW + dvc * 128: kc * SW + (dvc + 1) * 128],
                                    SRlo[:, kc * SW + dvc * 128: kc * SW + (dvc + 1) * 128],
                                    qfhi[:, kc * 512:(kc + 1) * 512],
                                    qflo[:, kc * 512:(kc + 1) * 512],
                                    start=(kc == 0), stop=(kc == NDK - 1))
                        mmv = p.inc(h, "T")
                        p.wait(v, "T", mmv)
                        dstn = numst[:, dvc * 512:(dvc + 1) * 512]
                        h = v.tensor_tensor(dstn, dstn, psC[pp][:], ALU.add)
                        psc_evac.append(p.inc(h, "V"))
                    for kc in range(NDK):
                        h = mm3(psD2[:],
                                SRhi[:, kc * SW + 1024: kc * SW + 1025],
                                SRlo[:, kc * SW + 1024: kc * SW + 1025],
                                qfhi[:, kc * 512:(kc + 1) * 512],
                                qflo[:, kc * 512:(kc + 1) * 512],
                                start=(kc == 0), stop=(kc == NDK - 1))
                    mmv = p.inc(h, "T")
                    p.wait(v, "T", mmv)
                    h = v.tensor_tensor(denrow[:], denrow[:], psD2[:], ALU.add)
                    fence(h, "V", v)
                    h = v.tensor_scalar(denrow[:], denrow[:], DEN_EPS, None, ALU.add)
                    fence(h, "V", v)
                    h = v.reciprocal(denrow[:], denrow[:])
                    dden = p.inc(h, "V")
                    p.wait(t, "V", dden)
                    h = t.matmul(psB[:], ones_r32[:], denrow[:], start=True, stop=True)
                    bden = p.inc(h, "T")
                    p.wait(v, "T", bden)
                    for dvc in range(NDK):
                        h = v.tensor_tensor(ndt[:], numst[:, dvc * 512:(dvc + 1) * 512], psB[:],
                                            ALU.mult)
                        fence(h, "V", v)
                        h = split2(ndhi[:, dvc * 512:(dvc + 1) * 512],
                                   ndlo[:, dvc * 512:(dvc + 1) * 512], ndt[:])
                        p.inc(h, "V")
                    nds = p.val["V"]
                    p.wait(t, "V", nds)
                    proj_evac = {0: None, 1: None}
                    for ts4 in range(4):
                        for nh in range(2):
                            pp = nh % 2
                            if proj_evac[pp] is not None:
                                p.wait(t, "V", proj_evac[pp])
                            for dvc in range(NDK):
                                h = mm3(psC[pp][:],
                                        ndhi[:, dvc * 512 + ts4 * 128: dvc * 512 + (ts4 + 1) * 128],
                                        ndlo[:, dvc * 512 + ts4 * 128: dvc * 512 + (ts4 + 1) * 128],
                                        wphi[:, dvc * D + nh * 512: dvc * D + (nh + 1) * 512],
                                        wplo[:, dvc * D + nh * 512: dvc * D + (nh + 1) * 512],
                                        start=(dvc == 0), stop=(dvc == NDK - 1))
                            mmv = p.inc(h, "T")
                            if cf > 0 or ts4 > 0 or nh > 0:
                                pass
                            p.wait(v, "T", mmv)
                            if rstg_dma:
                                p.wait(v, "qout", rstg_dma)
                            h = v.scalar_tensor_tensor(rstg[:, nh * 512:(nh + 1) * 512], psC[pp][:],
                                                       1.0, bpb[:, nh * 512:(nh + 1) * 512],
                                                       ALU.mult, ALU.add)
                            fence(h, "V", v)
                            h = v.tensor_tensor(rstg[:, nh * 512:(nh + 1) * 512],
                                                rstg[:, nh * 512:(nh + 1) * 512],
                                                xrt[:, ts4 * D + nh * 512: ts4 * D + (nh + 1) * 512],
                                                ALU.add)
                            vv = p.inc(h, "V")
                            proj_evac[pp] = vv
                        p.wait(g, "V", vv)
                        h = g.dma_start(out=s_r[base + ts4 * 128: base + (ts4 + 1) * 128, :],
                                        in_=rstg[:])
                        rstg_dma = p.inc(h, "qout", 16)
                    ld_prev_T = p.val["T"]
                    ld_prev_V = p.val["V"]
                phase_barrier()

        with ExitStack() as ph:
            upt = sbuf(ph, "q3up", [128, T], F32)
            gtt = sbuf(ph, "q3gt", [128, T], F32)
            sig = sbuf(ph, "q3sig", [128, T], F32)
            g2v = sbuf(ph, "q3g2", [128, T], F32)
            q16 = sbuf(ph, "q3i16", [128, T], I16)
            red2 = sbuf(ph, "q3red", [128, 1], F32)
            q16_dma = {}
            for j in range(DI // 128):
                rows = slice(j * 128, (j + 1) * 128)
                if j > 0:
                    p.wait(q, "V")
                    p.wait(q, "S")
                h = q.dma_start(out=upt[:], in_=s_upraw[rows, :])
                p.inc(h, "qin", 16)
                h = q.dma_start(out=gtt[:], in_=s_gateraw[rows, :])
                lv = p.inc(h, "qin", 16)
                p.wait(v, "qin", lv)
                h = v.tensor_scalar(upt[:], upt[:], slot(SL_RS2), MAGIC, ALU.mult, ALU.add)
                fence(h, "V", v)
                h = v.tensor_scalar(upt[:], upt[:], MAGIC, -32768.0, ALU.subtract, ALU.max)
                fence(h, "V", v)
                h = v.tensor_scalar(upt[:], upt[:], 32767.0, None, ALU.min)
                fence(h, "V", v)
                if (j - 1, 1) in q16_dma:
                    p.wait(v, "qout", q16_dma[(j - 1, 1)])
                h = v.tensor_copy(q16[:], upt[:])
                vv = p.inc(h, "V")
                p.wait(g, "V", vv)
                h = g.dma_start(out=s_upq[rows, :], in_=q16[:])
                q16_dma[(j, 0)] = p.inc(h, "qout", 16)
                h = v.tensor_scalar(gtt[:], gtt[:], slot(SL_RS3), MAGIC, ALU.mult, ALU.add)
                fence(h, "V", v)
                h = v.tensor_scalar(gtt[:], gtt[:], MAGIC, -32768.0, ALU.subtract, ALU.max)
                fence(h, "V", v)
                h = v.tensor_scalar(gtt[:], gtt[:], 32767.0, None, ALU.min)
                vv = p.inc(h, "V")
                p.wait(v, "V", vv)
                p.wait(v, "qout", q16_dma[(j, 0)])
                h = v.tensor_copy(q16[:], gtt[:])
                vv = p.inc(h, "V")
                p.wait(g, "V", vv)
                h = g.dma_start(out=s_gateq[rows, :], in_=q16[:])
                q16_dma[(j, 1)] = p.inc(h, "qout", 16)
                p.wait(s, "V", vv)
                h = s.activation(sig[:], gtt[:], AF.Sigmoid, bias=0.0, scale=slot(SL_S3))
                sv = p.inc(h, "S")
                p.wait(s, "S", sv)
                h = s.activation(sig[:], sig[:], AF.Copy, bias=MAGIC, scale=65536.0)
                sv = p.inc(h, "S")
                p.wait(v, "S", sv)
                h = v.tensor_scalar(sig[:], sig[:], MAGIC, 65535.0, ALU.subtract, ALU.min)
                fence(h, "V", v)
                h = v.tensor_tensor(g2v[:], gtt[:], sig[:], ALU.mult)
                fence(h, "V", v)
                h = v.tensor_reduce(red2[:], g2v[:], AX.X, ALU.max, apply_absolute_value=True)
                fence(h, "V", v)
                h = v.tensor_tensor(macc[:, 3:4], macc[:, 3:4], red2[:], ALU.max)
                p.inc(h, "V")
            vv = part_max_to_col(0, 3)
            p.wait(g, "V", vv)
            h = g.dma_start(out=cc4_in[:, :], in_=scrow[0:1, 0:16])
            p.inc(h, "qout", 16)
            p.wait(g, "qout")
            h = g.collective_compute("AllReduce", ALU.max, replica_groups=GRP_ALL,
                                     ins=[cc4_in.opt()], outs=[cc4_out.opt()])
            p.inc(h, "CC")
            phase_barrier()

        SL_SG2, SL_T1, SL_T2 = 11, 12, 13
        p.wait(q, "CC")
        with nc.sbuf_tensor("m4row", [1, 16], F32) as mrow:
            bcast_cc_row(cc4_out[:, :], mrow[:], slot(SL_S4), 1)
            h = v.tensor_scalar(slot(SL_SG2), slot(SL_S3), 1.0 / 65536.0, None, ALU.mult)
            fence(h, "V", v)
            h = v.tensor_tensor(slot(SL_S4), slot(SL_S4), slot(SL_SG2), ALU.mult)
            fence(h, "V", v)
            h = v.tensor_scalar(slot(SL_S4), slot(SL_S4), 1.0 / 32767.0, 1e-12, ALU.mult, ALU.max)
            fence(h, "V", v)
            h = v.reciprocal(slot(SL_T1), slot(SL_S4))
            fence(h, "V", v)
            h = v.tensor_tensor(slot(SL_C4), slot(SL_SG2), slot(SL_T1), ALU.mult)
            fence(h, "V", v)
            phase_barrier()

        with ExitStack() as ph:
            gq16 = sbuf(ph, "b3gq", [128, T], I16)
            uq16 = sbuf(ph, "b3uq", [128, T], I16)
            sig = sbuf(ph, "b3sig", [128, T], F32)
            g2v = sbuf(ph, "b3g2", [128, T], F32)
            ov = sbuf(ph, "b3ov", [128, T], F32)
            o16 = sbuf(ph, "b3o16", [128, T], I16)
            red2 = sbuf(ph, "b3red", [128, 1], F32)
            o16_dma = 0
            for j in range(DI // 128):
                rows = slice(j * 128, (j + 1) * 128)
                if j > 0:
                    p.wait(q, "V")
                    p.wait(q, "S")
                h = q.dma_start(out=gq16[:], in_=s_gateq[rows, :])
                p.inc(h, "qin", 16)
                h = q.dma_start(out=uq16[:], in_=s_upq[rows, :])
                lv = p.inc(h, "qin", 16)
                p.wait(s, "qin", lv)
                h = s.activation(sig[:], gq16[:], AF.Sigmoid, bias=0.0, scale=slot(SL_S3))
                sv = p.inc(h, "S")
                p.wait(s, "S", sv)
                h = s.activation(sig[:], sig[:], AF.Copy, bias=MAGIC, scale=65536.0)
                sv = p.inc(h, "S")
                p.wait(v, "S", sv)
                h = v.tensor_scalar(sig[:], sig[:], MAGIC, 65535.0, ALU.subtract, ALU.min)
                fence(h, "V", v)
                h = v.tensor_tensor(g2v[:], gq16[:], sig[:], ALU.mult)
                fence(h, "V", v)
                h = v.tensor_scalar(g2v[:], g2v[:], slot(SL_C4), MAGIC, ALU.mult, ALU.add)
                fence(h, "V", v)
                h = v.tensor_scalar(g2v[:], g2v[:], MAGIC, -32768.0, ALU.subtract, ALU.max)
                fence(h, "V", v)
                h = v.tensor_scalar(g2v[:], g2v[:], 32767.0, None, ALU.min)
                fence(h, "V", v)
                if o16_dma:
                    p.wait(v, "qout", o16_dma)
                h = v.tensor_copy(o16[:], g2v[:])
                vv = p.inc(h, "V")
                p.wait(g, "V", vv)
                h = g.dma_start(out=s_g2q[rows, :], in_=o16[:])
                o16_dma = p.inc(h, "qout", 16)
                h = v.tensor_tensor(ov[:], g2v[:], uq16[:], ALU.mult)
                fence(h, "V", v)
                h = v.tensor_reduce(red2[:], ov[:], AX.X, ALU.max, apply_absolute_value=True)
                fence(h, "V", v)
                h = v.tensor_tensor(macc[:, 4:5], macc[:, 4:5], red2[:], ALU.max)
                p.inc(h, "V")
            vv = part_max_to_col(0, 4)
            p.wait(g, "V", vv)
            h = g.dma_start(out=cc5_in[:, :], in_=scrow[0:1, 0:16])
            p.inc(h, "qout", 16)
            p.wait(g, "qout")
            h = g.collective_compute("AllReduce", ALU.max, replica_groups=GRP_ALL,
                                     ins=[cc5_in.opt()], outs=[cc5_out.opt()])
            p.inc(h, "CC")
            phase_barrier()

        p.wait(q, "CC")
        with nc.sbuf_tensor("m5row", [1, 16], F32) as mrow:
            bcast_cc_row(cc5_out[:, :], mrow[:], slot(SL_S5), 1)
            h = v.tensor_tensor(slot(SL_T2), slot(SL_S4), slot(SL_S2), ALU.mult)
            fence(h, "V", v)
            h = v.tensor_tensor(slot(SL_S5), slot(SL_S5), slot(SL_T2), ALU.mult)
            fence(h, "V", v)
            h = v.tensor_scalar(slot(SL_S5), slot(SL_S5), 1.0 / 32767.0, 1e-12, ALU.mult, ALU.max)
            fence(h, "V", v)
            h = v.reciprocal(slot(SL_T1), slot(SL_S5))
            fence(h, "V", v)
            h = v.tensor_tensor(slot(SL_C5), slot(SL_T2), slot(SL_T1), ALU.mult)
            fence(h, "V", v)
            phase_barrier()

        with ExitStack() as ph:
            wdthi = sbuf(ph, "p7wdh", [128, 24 * D], BF16)
            wdtlo = sbuf(ph, "p7wdl", [128, 24 * D], BF16)
            g2qt = sbuf(ph, "p7g2", [128, 24 * 512], I16)
            uqt = sbuf(ph, "p7uq", [128, 24 * 512], I16)
            rt = sbuf(ph, "p7rt", [128, 4 * D], F32)
            ot = sbuf(ph, "p7ot", [128, 512], F32)
            ohi = g2qt.bitcast(BF16)
            olo = uqt.bitcast(BF16)
            ystg = sbuf(ph, "p7y", [128, D], F32)
            psP = [psum(ph, f"p7ps{i}", [128, 512], F32) for i in range(2)]

            for ic in range(24):
                h = q.dma_start(out=wdthi[:, ic * D:(ic + 1) * D],
                                in_=s_wdthi[:, ic * 128:(ic + 1) * 128], transpose=True)
                p.inc(h, "qin", 16)
                h = q.dma_start(out=wdtlo[:, ic * D:(ic + 1) * D],
                                in_=s_wdtlo[:, ic * 128:(ic + 1) * 128], transpose=True)
                p.inc(h, "qin", 16)
            wd_lds = p.val["qin"]
            p.wait(t, "qin", wd_lds)

            ystg_dma = 0
            grp_T_done = 0
            grp_V_done = 0
            for gq in range(NCA):
                base = gq * CA
                if gq > 0:
                    p.wait(q, "T", grp_T_done)
                    p.wait(q, "V", grp_V_done)
                h = q.dma_start(out=g2qt[:].rearrange("p (a n) -> p a n", a=24),
                                in_=s_g2q[:, base:base + CA].rearrange("(a p) n -> p a n", p=128))
                p.inc(h, "qin", 16)
                h = q.dma_start(out=uqt[:].rearrange("p (a n) -> p a n", a=24),
                                in_=s_upq[:, base:base + CA].rearrange("(a p) n -> p a n", p=128))
                p.inc(h, "qin", 16)
                h = q.dma_start(out=rt[:].rearrange("p (a n) -> p a n", a=4),
                                in_=s_r[base:base + CA, :].rearrange("(a p) n -> p a n", p=128))
                lds = p.inc(h, "qin", 16)
                p.wait(v, "qin", lds)
                if gq > 0:
                    p.wait(v, "T", grp_T_done)
                for ic in range(24):
                    sl = slice(ic * 512, (ic + 1) * 512)
                    h = v.tensor_tensor(ot[:], g2qt[:, sl], uqt[:, sl], ALU.mult)
                    fence(h, "V", v)
                    h = v.tensor_scalar(ot[:], ot[:], slot(SL_C5), MAGIC, ALU.mult, ALU.add)
                    fence(h, "V", v)
                    h = v.tensor_scalar(ot[:], ot[:], MAGIC, -32768.0, ALU.subtract, ALU.max)
                    fence(h, "V", v)
                    h = v.tensor_scalar(ot[:], ot[:], 32767.0, None, ALU.min)
                    fence(h, "V", v)
                    h = split2(ohi[:, sl], olo[:, sl], ot[:])
                    p.inc(h, "V")
                ov_done = p.val["V"]
                p.wait(t, "V", ov_done)
                p7_evac = {0: None, 1: None}
                for tc in range(4):
                    for nh in range(2):
                        pp = nh
                        if p7_evac[pp] is not None:
                            p.wait(t, "V", p7_evac[pp])
                        for ic in range(24):
                            h = mm3(psP[pp][:],
                                    ohi[:, ic * 512 + tc * 128: ic * 512 + (tc + 1) * 128],
                                    olo[:, ic * 512 + tc * 128: ic * 512 + (tc + 1) * 128],
                                    wdthi[:, ic * D + nh * 512: ic * D + (nh + 1) * 512],
                                    wdtlo[:, ic * D + nh * 512: ic * D + (nh + 1) * 512],
                                    start=(ic == 0), stop=(ic == 23))
                        mmv = p.inc(h, "T")
                        p.wait(v, "T", mmv)
                        if ystg_dma:
                            p.wait(v, "qout", ystg_dma)
                        h = v.scalar_tensor_tensor(ystg[:, nh * 512:(nh + 1) * 512], psP[pp][:],
                                                   slot(SL_S5),
                                                   rt[:, tc * D + nh * 512: tc * D + (nh + 1) * 512],
                                                   ALU.mult, ALU.add)
                        vv = p.inc(h, "V")
                        p7_evac[pp] = vv
                    p.wait(g, "V", vv)
                    h = g.dma_start(out=y_out[base + tc * 128: base + (tc + 1) * 128, :], in_=ystg[:])
                    ystg_dma = p.inc(h, "qout", 16)
                grp_T_done = p.val["T"]
                grp_V_done = p.val["V"]
            phase_barrier()

        if dbg == 2:
            for nm, srcap in (("upraw", s_upraw[0:128, :]), ("upq", s_upq[0:128, :]),
                              ("gateq", s_gateq[0:128, :]), ("g2q", s_g2q[0:128, :]),
                              ("r", s_r[0:128, :]), ("num", s_num[0:128, :]),
                              ("wdh", s_wdthi[0:128, :])):
                h = g.dma_start(out=t_pr[nm][:, :], in_=srcap)
                p.inc(h, "qout", 16)
            p.wait(g, "qout")

        h = g.dma_start(out=t_slots[:, :], in_=slots[:])
        p.inc(h, "qout", 16)
        h = g.dma_start(out=t_macc[:, :], in_=macc[:])
        p.inc(h, "qout", 16)
        for i, cco in enumerate((cc1_out, cc23_out, cc4_out, cc5_out)):
            h = g.dma_start(out=t_cc[i:i + 1, :], in_=cco[:, :])
            p.inc(h, "qout", 16)

        h = t.matmul(ps_red[:, 0:16], ones_r32[:], scrow[0:1, 0:16], start=True, stop=True)
        p.inc(h, "T")
        h = v.engine_nop()
        p.inc(h, "V")
        h = s.copy(red1[:, :], red1[:, :])
        p.inc(h, "S")
        p.wait(g, "T")
        p.wait(g, "V")
        p.wait(g, "S")
        p.wait(g, "qin")
        p.wait(g, "qout")
        p.wait(g, "CC")

    return nc


def kernel(L0_attn_x, L0_attn_w_qkv, L0_attn_b_qkv, L0_attn_w_gate, L0_attn_b_gate,
           L0_attn_w_proj, L0_attn_b_proj, L0_attn_ln_g, L0_attn_ln_b,
           L0_ffn_w_gate, L0_ffn_w_up, L0_ffn_w_down):
    x = np.ascontiguousarray(np.asarray(L0_attn_x, dtype=np.float32))
    B, TT, DD = x.shape
    nc = build_nc()
    in_maps = []
    for core in range(8):
        b, hh = core // 2, core % 2
        in_maps.append({
            "x": np.ascontiguousarray(x[b, hh * T:(hh + 1) * T, :]),
            "w_qkv": np.asarray(L0_attn_w_qkv, dtype=np.float32),
            "b_qkv": np.asarray(L0_attn_b_qkv, dtype=np.float32),
            "w_gate": np.asarray(L0_attn_w_gate, dtype=np.float32),
            "b_gate": np.asarray(L0_attn_b_gate, dtype=np.float32),
            "w_proj": np.asarray(L0_attn_w_proj, dtype=np.float32),
            "b_proj": np.asarray(L0_attn_b_proj, dtype=np.float32),
            "ln_g": np.asarray(L0_attn_ln_g, dtype=np.float32),
            "ln_b": np.asarray(L0_attn_ln_b, dtype=np.float32),
            "fw_gate": np.asarray(L0_ffn_w_gate, dtype=np.float32),
            "fw_up": np.asarray(L0_ffn_w_up, dtype=np.float32),
            "fw_down": np.asarray(L0_ffn_w_down, dtype=np.float32),
            "hflag": np.full((128, 1), float(hh), dtype=np.float32),
        })
    res = run_bass_kernel_spmd(nc, in_maps, core_ids=list(range(8)))
    out = np.empty((B, TT, DD), dtype=np.float32)
    for core in range(8):
        b, hh = core // 2, core % 2
        out[b, hh * T:(hh + 1) * T, :] = res.results[core]["y"]
    return out


# version 08
# speedup vs baseline: 1.0510x; 1.0510x over previous
import numpy as np
from contextlib import ExitStack
import concourse.bass as bass
import concourse.mybir as mybir
from concourse.bass_utils import run_bass_kernel_spmd

F32 = mybir.dt.float32
BF16 = mybir.dt.bfloat16
I16 = mybir.dt.int16
I32 = mybir.dt.int32
ALU = mybir.AluOpType
AF = mybir.ActivationFunctionType
AX = mybir.AxisListType

MAGIC = 12582912.0
D = 1024
DI = 3072
T = 4096
NDK = D // 128
LN_EPS = 1e-5
DEN_EPS = 1e-6
CA = 512
NCA = T // CA
GRP_ALL = [list(range(8))]
GRP_PAIR = [[0, 1], [2, 3], [4, 5], [6, 7]]

SL_RS1, SL_S1 = 0, 1
SL_RS2, SL_S2 = 2, 3
SL_RS3, SL_S3 = 4, 5
SL_C4, SL_S4 = 6, 7
SL_C5, SL_S5 = 8, 9
SL_HINV = 10


class P:

    def __init__(self, nc, stack):
        self.nc = nc
        self.stack = stack
        self.sems = {}
        self.val = {}
        for name in ("qin", "qout", "T", "V", "S", "G", "CC", "BOOT"):
            self.sems[name] = stack.enter_context(nc.semaphore(name))
            self.val[name] = 0

    def inc(self, handle, sem, n=1):
        handle.then_inc(self.sems[sem], n)
        self.val[sem] += n
        return self.val[sem]

    def wait(self, eng, sem, v=None):
        eng.wait_ge(self.sems[sem], self.val[sem] if v is None else v)


def build_nc(dbg=False):
    nc = bass.Bass("TRN2", target_bir_lowering=False, debug=False)

    x_in = nc.dram_tensor("x", [T, D], F32, kind="ExternalInput").ap()
    w_qkv = nc.dram_tensor("w_qkv", [D, 3 * D], F32, kind="ExternalInput").ap()
    b_qkv = nc.dram_tensor("b_qkv", [3 * D], F32, kind="ExternalInput").ap()
    w_gate = nc.dram_tensor("w_gate", [D, D], F32, kind="ExternalInput").ap()
    b_gate = nc.dram_tensor("b_gate", [D], F32, kind="ExternalInput").ap()
    w_proj = nc.dram_tensor("w_proj", [D, D], F32, kind="ExternalInput").ap()
    b_proj = nc.dram_tensor("b_proj", [D], F32, kind="ExternalInput").ap()
    ln_g = nc.dram_tensor("ln_g", [D], F32, kind="ExternalInput").ap()
    ln_b = nc.dram_tensor("ln_b", [D], F32, kind="ExternalInput").ap()
    fw_gate = nc.dram_tensor("fw_gate", [DI, D], F32, kind="ExternalInput").ap()
    fw_up = nc.dram_tensor("fw_up", [DI, D], F32, kind="ExternalInput").ap()
    fw_down = nc.dram_tensor("fw_down", [D, DI], F32, kind="ExternalInput").ap()
    hflag = nc.dram_tensor("hflag", [128, 1], F32, kind="ExternalInput").ap()

    y_out = nc.dram_tensor("y", [T, D], F32, kind="ExternalOutput").ap()
    t_slots = nc.dram_tensor("t_slots", [128, 16], F32, kind="ExternalOutput").ap()
    t_macc = nc.dram_tensor("t_macc", [128, 8], F32, kind="ExternalOutput").ap()
    t_cc = nc.dram_tensor("t_cc", [4, 16], F32, kind="ExternalOutput").ap()

    def scratch(name, shape, dtype):
        if dbg == 1:
            return nc.dram_tensor(name, shape, dtype, kind="ExternalOutput").ap()
        return nc.dram_tensor(name, shape, dtype).ap()

    s_xn = scratch("s_xn", [T, D], F32)
    s_qhi = scratch("s_qhi", [D, T], BF16)
    s_qlo = scratch("s_qlo", [D, T], BF16)
    s_khi = scratch("s_khi", [D, T], BF16)
    s_klo = scratch("s_klo", [D, T], BF16)
    s_vhi = scratch("s_vhi", [D, T], BF16)
    s_vlo = scratch("s_vlo", [D, T], BF16)
    s_num = scratch("s_num", [D, T], F32)
    s_den = scratch("s_den", [1, T], F32)
    s_upraw = scratch("s_upraw", [DI, T], F32)
    s_gateraw = scratch("s_gateraw", [DI, T], F32)
    s_upq = scratch("s_upq", [DI, T], I16)
    s_gateq = scratch("s_gateq", [DI, T], I16)
    s_g2q = scratch("s_g2q", [DI, T], I16)
    s_wdthi = scratch("s_wdthi", [D, DI], BF16)
    s_wdtlo = scratch("s_wdtlo", [D, DI], BF16)
    s_r = scratch("s_r", [T, D], F32)
    t_dbg = nc.dram_tensor("t_dbg", [128, 64], F32, kind="ExternalOutput").ap() if dbg else None
    t_pr = {}
    if dbg == 2:
        for nm, shape, dt_ in (("upraw", [128, T], F32), ("upq", [128, T], I16),
                               ("gateq", [128, T], I16), ("g2q", [128, T], I16),
                               ("r", [128, D], F32), ("num", [128, T], F32),
                               ("wdh", [128, DI], BF16)):
            t_pr[nm] = nc.dram_tensor("t_pr_" + nm, shape, dt_, kind="ExternalOutput").ap()
    t_st = nc.dram_tensor("t_st", [T // 128 * 128, 8], F32, kind="ExternalOutput").ap() if dbg else None
    t_S = nc.dram_tensor("t_S", [128, NDK * 1032], F32, kind="ExternalOutput").ap() if dbg else None

    cc1_in = nc.dram_tensor("cc1_in", [1, 16], F32).ap()
    cc1_out = nc.dram_tensor("cc1_out", [1, 16], F32).ap()
    cc23_in = nc.dram_tensor("cc23_in", [1, 16], F32).ap()
    cc23_out = nc.dram_tensor("cc23_out", [1, 16], F32).ap()
    cc4_in = nc.dram_tensor("cc4_in", [1, 16], F32).ap()
    cc4_out = nc.dram_tensor("cc4_out", [1, 16], F32).ap()
    cc5_in = nc.dram_tensor("cc5_in", [1, 16], F32).ap()
    cc5_out = nc.dram_tensor("cc5_out", [1, 16], F32).ap()
    cc_s_in = nc.dram_tensor("cc_s_in", [128, NDK * 1032], F32).ap()
    cc_s_out = nc.dram_tensor("cc_s_out", [128, NDK * 1032], F32).ap()

    with ExitStack() as stack:
        p = P(nc, stack)
        stack.enter_context(nc.Block())
        t, v, s, g, q = nc.tensor, nc.vector, nc.scalar, nc.gpsimd, nc.sync

        def sbuf(st, name, shape, dtype):
            return st.enter_context(nc.sbuf_tensor(name, shape, dtype))

        def psum(st, name, shape, dtype=F32):
            return st.enter_context(nc.psum_tensor(name, shape, dtype))

        sem_nums = [h.num for h in p.sems.values()]
        boot_rng = range(min(sem_nums), max(sem_nums) + 1)
        g.dma_reset(boot_rng)
        g.sem_clear(boot_rng)
        h = g.engine_nop()
        p.inc(h, "BOOT")
        for eng in (t, v, s, q):
            eng.nop(cycle_cnt=30000)
            p.wait(eng, "BOOT", 1)

        ones_r32 = sbuf(stack, "ones_r32", [1, 128], F32)
        ones_c16 = sbuf(stack, "ones_c16", [128, 1], BF16)
        ident32 = sbuf(stack, "ident32", [128, 128], F32)
        iden16 = sbuf(stack, "iden16", [128, 128], BF16)
        iota_f = sbuf(stack, "iota_f", [128, 512], F32)
        pvec_f = sbuf(stack, "pvec_f", [128, 1], F32)
        slots = sbuf(stack, "slots", [128, 16], F32)
        macc = sbuf(stack, "macc", [128, 8], F32)
        hfl = sbuf(stack, "hfl", [128, 1], F32)
        bq_sb = sbuf(stack, "bq_sb", [128, 24], F32)
        bg_sb = sbuf(stack, "bg_sb", [128, 8], F32)
        scrow = sbuf(stack, "scrow", [2, 16], F32)
        epsc = sbuf(stack, "epsc", [128, 1], F32)
        red1 = sbuf(stack, "red1", [2, 1], F32)
        ps_red = psum(stack, "ps_red", [128, 512], F32)
        dbg_sb = sbuf(stack, "dbg_sb", [128, 64], F32)

        def slot(col):
            return slots[:, col:col + 1]

        def fence(h, sem, eng):
            val = p.inc(h, sem)
            p.wait(eng, sem, val)

        def phase_barrier():
            for eng in (t, v, s, g, q):
                for semname in ("T", "V", "S", "G", "qin", "qout", "CC"):
                    if p.val[semname]:
                        p.wait(eng, semname)

        def split2(dst_hi, dst_lo, src, eng=None, sem="V"):
            e = eng or v
            h = e.tensor_copy(dst_hi, src)
            fence(h, sem, e)
            return e.tensor_tensor(dst_lo, src, dst_hi, ALU.subtract)

        def mm3(out_ps, lh_hi, lh_lo, rh_hi, rh_lo, start, stop):
            t.matmul(out_ps, lh_hi, rh_hi, start=start, stop=False)
            t.matmul(out_ps, lh_hi, rh_lo, start=False, stop=False)
            return t.matmul(out_ps, lh_lo, rh_hi, start=False, stop=stop)

        def part_max_to_col(col, maccol):
            p.wait(t, "V")
            h = t.transpose(ps_red[0:1, 0:128], macc[:, maccol:maccol + 1], ident32[:])
            mv = p.inc(h, "T")
            p.wait(v, "T", mv)
            h = v.tensor_reduce(red1[0:1, :], ps_red[0:1, 0:128], AX.X, ALU.max)
            fence(h, "V", v)
            h = v.tensor_copy(scrow[0:1, col:col + 1], red1[0:1, :])
            return p.inc(h, "V")

        def bcast_cc_row(cc_out_ap, tmp_row, dst_sl, ncols):
            p.wait(g, "CC")
            h = g.dma_start(out=tmp_row, in_=cc_out_ap)
            lv = p.inc(h, "qin", 16)
            p.wait(t, "qin", lv)
            h = t.matmul(ps_red[:, 0:16], ones_r32[:], tmp_row, start=True, stop=True)
            mv = p.inc(h, "T")
            p.wait(v, "T", mv)
            h = v.tensor_copy(dst_sl, ps_red[:, 0:ncols])
            val = p.inc(h, "V")
            p.wait(v, "V", val)
            return val

        h = q.dma_start(out=hfl[:], in_=hflag[:, :])
        p.inc(h, "qin", 16)
        h = q.dma_start(out=bq_sb[:], in_=b_qkv.rearrange("(a p) -> p a", p=128),
                        allow_slow_non_contiguous=True)
        p.inc(h, "qin", 16)
        h = q.dma_start(out=bg_sb[:], in_=b_gate.rearrange("(a p) -> p a", p=128),
                        allow_slow_non_contiguous=True)
        p.inc(h, "qin", 16)

        with nc.sbuf_tensor("iota_i", [128, 512], I32) as iota_i, \
             nc.sbuf_tensor("iotp_i", [128, 1], I32) as iotp_i:
            g.iota(iota_i[:], pattern=[[1, 512]], channel_multiplier=0)
            g.iota(iotp_i[:], pattern=[[0, 1]], channel_multiplier=1)
            g.memset(ones_r32[:], 1.0)
            g.memset(ones_c16[:], 1.0)
            g.memset(slots[:], 0.0)
            g.memset(scrow[:, :], 0.0)
            g.memset(epsc[:], LN_EPS)
            h = g.memset(macc[:], 0.0)
            p.inc(h, "G")
            p.wait(v, "G")
            p.wait(v, "qin")
            v.tensor_copy(iota_f[:], iota_i[:])
            h = v.tensor_copy(pvec_f[:], iotp_i[:])
            fence(h, "V", v)
            h = v.tensor_scalar(ident32[:], iota_f[:, 0:128], pvec_f[:], None, ALU.is_equal)
            fence(h, "V", v)
            v.tensor_copy(iden16[:], ident32[:])
            h = v.tensor_scalar(slot(SL_HINV), hfl[:], -1.0, 1.0, ALU.mult, ALU.add)
            p.inc(h, "V")

        st_a01 = ExitStack()
        xn2fm_hi = sbuf(st_a01, "xn2fm_hi", [128, NDK * T], BF16)
        xn2fm_lo = sbuf(st_a01, "xn2fm_lo", [128, NDK * T], BF16)

        with ExitStack() as ph:
            gb_b = sbuf(ph, "gb_b", [128, D], F32)
            bb_b = sbuf(ph, "bb_b", [128, D], F32)
            grow = sbuf(ph, "grow", [1, D], F32)
            brow = sbuf(ph, "brow", [1, D], F32)
            xt = [sbuf(ph, f"a0x{i}", [128, D], F32) for i in range(2)]
            st8 = sbuf(ph, "a0st", [128, 8], F32)
            xc = sbuf(ph, "a0xc", [128, D], F32)
            sq = sbuf(ph, "a0sq", [128, D], F32)
            xn_t2 = [sbuf(ph, f"a0xn{i}", [128, D], F32) for i in range(2)]
            xn2 = sbuf(ph, "a0xn2", [128, D], F32)
            xn2hi = sbuf(ph, "a0hi", [128, D], BF16)
            xn2lo = sbuf(ph, "a0lo", [128, D], BF16)
            red = sbuf(ph, "a0red", [128, 1], F32)
            ps_thi = psum(ph, "ps_thi", [128, 1024], BF16)
            ps_tlo = psum(ph, "ps_tlo", [128, 1024], BF16)

            h = q.dma_start(out=grow[:], in_=ln_g.unsqueeze(0))
            p.inc(h, "qin", 16)
            h = q.dma_start(out=brow[:], in_=ln_b.unsqueeze(0))
            p.inc(h, "qin", 16)
            p.wait(t, "qin")
            for row, dst in ((grow, gb_b), (brow, bb_b)):
                for hf in range(2):
                    h = t.matmul(ps_red[:, 0:512], ones_r32[:], row[:, hf * 512:(hf + 1) * 512],
                                 start=True, stop=True)
                    mv = p.inc(h, "T")
                    p.wait(v, "T", mv)
                    h = v.tensor_copy(dst[:, hf * 512:(hf + 1) * 512], ps_red[:, 0:512])
                    vv = p.inc(h, "V")
                    p.wait(t, "V", vv)

            def ln_stats(src, mu_sl, rstd_sl):
                h = v.tensor_reduce(red[:], src, AX.X, ALU.add)
                fence(h, "V", v)
                h = v.tensor_scalar(mu_sl, red[:], 1.0 / D, None, ALU.mult)
                fence(h, "V", v)
                h = v.tensor_scalar(xc[:], src, mu_sl, None, ALU.subtract)
                vv = p.inc(h, "V")
                p.wait(s, "V", vv)
                h = s.activation(sq[:], xc[:], AF.Square, bias=0.0, scale=1.0)
                sv = p.inc(h, "S")
                p.wait(v, "S", sv)
                h = v.tensor_reduce(red[:], sq[:], AX.X, ALU.add)
                vv = p.inc(h, "V")
                p.wait(s, "V", vv)
                h = s.activation(rstd_sl, red[:], AF.Sqrt, bias=epsc[:], scale=1.0 / D)
                sv = p.inc(h, "S")
                p.wait(v, "S", sv)
                h = v.reciprocal(rstd_sl, rstd_sl)
                fence(h, "V", v)

            NT0 = T // 128
            ld_done = {}
            xn_dma = {}
            for c in range(2):
                h = q.dma_start(out=xt[c][:], in_=x_in[c * 128:(c + 1) * 128, :])
                ld_done[c] = p.inc(h, "qin", 16)
            for c in range(NT0):
                buf = xt[c % 2]
                xn_t = xn_t2[c % 2]
                p.wait(v, "qin", ld_done[c])
                ln_stats(buf[:], st8[:, 0:1], st8[:, 1:2])
                if c >= 2:
                    p.wait(v, "qout", xn_dma[c - 2])
                h = v.tensor_scalar(xn_t[:], xc[:], st8[:, 1:2], None, ALU.mult)
                buf_free = p.inc(h, "V")
                p.wait(v, "V", buf_free)
                if c + 2 < NT0:
                    p.wait(q, "V", buf_free)
                    h = q.dma_start(out=xt[c % 2][:], in_=x_in[(c + 2) * 128:(c + 3) * 128, :])
                    ld_done[c + 2] = p.inc(h, "qin", 16)
                h = v.tensor_reduce(red[:], xn_t[:], AX.X, ALU.max, apply_absolute_value=True)
                fence(h, "V", v)
                h = v.tensor_tensor(macc[:, 0:1], macc[:, 0:1], red[:], ALU.max)
                vv = p.inc(h, "V")
                p.wait(g, "V", vv)
                h = g.dma_start(out=s_xn[c * 128:(c + 1) * 128, :], in_=xn_t[:])
                xn_dma[c] = p.inc(h, "qout", 16)
                if dbg:
                    h = g.dma_start(out=t_st[c * 128:(c + 1) * 128, :], in_=st8[:])
                    p.inc(h, "qout", 16)
                ln_stats(xn_t[:], st8[:, 2:3], st8[:, 3:4])
                h = v.scalar_tensor_tensor(xn2[:], xc[:], st8[:, 3:4], gb_b[:], ALU.mult, ALU.mult)
                fence(h, "V", v)
                h = v.tensor_tensor(xn2[:], xn2[:], bb_b[:], ALU.add)
                fence(h, "V", v)
                h = split2(xn2hi[:], xn2lo[:], xn2[:])
                vv = p.inc(h, "V")
                p.wait(t, "V", vv)
                for dc in range(NDK):
                    t.transpose(ps_thi[:, dc * 128:(dc + 1) * 128], xn2hi[:, dc * 128:(dc + 1) * 128],
                                iden16[:])
                    h = t.transpose(ps_tlo[:, dc * 128:(dc + 1) * 128], xn2lo[:, dc * 128:(dc + 1) * 128],
                                    iden16[:])
                mv = p.inc(h, "T")
                p.wait(v, "T", mv)
                for dc in range(NDK):
                    v.tensor_copy(xn2fm_hi[:, dc * T + c * 128: dc * T + (c + 1) * 128],
                                  ps_thi[:, dc * 128:(dc + 1) * 128])
                    h = v.tensor_copy(xn2fm_lo[:, dc * T + c * 128: dc * T + (c + 1) * 128],
                                      ps_tlo[:, dc * 128:(dc + 1) * 128])
                vv = p.inc(h, "V")
                p.wait(t, "V", vv)

            vv = part_max_to_col(0, 0)
            p.wait(g, "V", vv)
            h = g.dma_start(out=cc1_in[:, :], in_=scrow[0:1, 0:16])
            p.inc(h, "qout", 16)
            p.wait(g, "qout")
            h = g.collective_compute("AllReduce", ALU.max, replica_groups=GRP_ALL,
                                     ins=[cc1_in.opt()], outs=[cc1_out.opt()])
            p.inc(h, "CC")
            phase_barrier()

        p.wait(q, "CC")
        with nc.sbuf_tensor("m1row", [1, 16], F32) as m1row:
            bcast_cc_row(cc1_out[:, :], m1row[:], slot(SL_S1), 1)
            if dbg:
                h = v.tensor_copy(dbg_sb[0:1, 0:16], m1row[:])
                p.inc(h, "V")
            h = v.tensor_scalar(slot(SL_S1), slot(SL_S1), 1.0 / 32767.0, 1e-12, ALU.mult, ALU.max)
            fence(h, "V", v)
            h = v.reciprocal(slot(SL_RS1), slot(SL_S1))
            fence(h, "V", v)
            phase_barrier()

        if dbg:
            h = v.tensor_copy(dbg_sb[:, 32:48], slots[:])
            h = v.tensor_copy(dbg_sb[0:2, 16:32], scrow[:, :])
            vv = p.inc(h, "V")
            p.wait(g, "V", vv)
            h = g.dma_start(out=t_dbg[:, :], in_=dbg_sb[:])
            p.inc(h, "qout", 16)

        with ExitStack() as ph:
            wtmp = [sbuf(ph, f"a1w{i}", [128, NDK * 128], F32) for i in range(2)]
            whi = [sbuf(ph, f"a1whi{i}", [128, NDK * 128], BF16) for i in range(2)]
            wlo = [sbuf(ph, f"a1wlo{i}", [128, NDK * 128], BF16) for i in range(2)]
            gate_buf = sbuf(ph, "a1gate", [128, T], F32)
            ktmp = sbuf(ph, "a1ktmp", [128, 512], F32)
            ktmp2 = sbuf(ph, "a1ktmp2", [128, 512], F32)
            etile = sbuf(ph, "a1e", [128, 512], F32)
            kptile = sbuf(ph, "a1kp", [128, 512], F32)
            obhi = [sbuf(ph, f"a1oh{i}", [128, T], BF16) for i in range(2)]
            oblo = [sbuf(ph, f"a1ol{i}", [128, T], BF16) for i in range(2)]
            ps_a1 = [psum(ph, f"ps_a1{i}", [128, 512], F32) for i in range(2)]

            sched = []
            for j in range(NDK):
                sched.append(("gate", j))
                sched.append(("k", j))
            for j in range(NDK):
                sched.append(("q", j))
            for j in range(NDK):
                sched.append(("v", j))

            def wsrc(role, j):
                if role == "gate":
                    return w_gate[:, j * 128:(j + 1) * 128]
                off = {"q": 0, "k": D, "v": 2 * D}[role] + j * 128
                return w_qkv[:, off:off + 128]

            def bias_col(role, j):
                if role == "gate":
                    return bg_sb[:, j:j + 1]
                a = {"q": 0, "k": 8, "v": 16}[role] + j
                return bq_sb[:, a:a + 1]

            ostore = {"k": (s_khi, s_klo), "q": (s_qhi, s_qlo), "v": (s_vhi, s_vlo)}

            ld_w = {}
            split_done = {}
            mm_done = {}
            evac_done = []
            ob_dma = {}
            gate_read_done = {}
            n_uses = 0

            for c in range(2):
                h = q.dma_start(out=wtmp[c][:].rearrange("p (a n) -> p a n", a=NDK),
                                in_=wsrc(*sched[c]).rearrange("(a p) n -> p a n", p=128))
                ld_w[c] = p.inc(h, "qin", 16)

            for c, (role, j) in enumerate(sched):
                par = c % 2
                p.wait(v, "qin", ld_w[c])
                if c >= 2:
                    p.wait(v, "T", mm_done[c - 2])
                h = split2(whi[par][:], wlo[par][:], wtmp[par][:])
                split_done[c] = p.inc(h, "V")
                if c + 2 < len(sched):
                    p.wait(q, "V", split_done[c])
                    h = q.dma_start(out=wtmp[par][:].rearrange("p (a n) -> p a n", a=NDK),
                                    in_=wsrc(*sched[c + 2]).rearrange("(a p) n -> p a n", p=128))
                    ld_w[c + 2] = p.inc(h, "qin", 16)
                if role != "gate" and (c - 2) in ob_dma:
                    p.wait(v, "qout", ob_dma[c - 2])
                p.wait(t, "V", split_done[c])
                for nt in range(8):
                    pp = n_uses % 2
                    if n_uses >= 2:
                        sem, val = evac_done[n_uses - 2]
                        p.wait(t, sem, val)
                    for kc in range(NDK):
                        h = mm3(ps_a1[pp][:],
                                whi[par][:, kc * 128:(kc + 1) * 128],
                                wlo[par][:, kc * 128:(kc + 1) * 128],
                                xn2fm_hi[:, kc * T + nt * 512: kc * T + (nt + 1) * 512],
                                xn2fm_lo[:, kc * T + nt * 512: kc * T + (nt + 1) * 512],
                                start=(kc == 0), stop=(kc == NDK - 1))
                    mmv = p.inc(h, "T")
                    if nt == 7:
                        mm_done[c] = mmv
                    sl = slice(nt * 512, (nt + 1) * 512)
                    if role == "gate":
                        p.wait(s, "T", mmv)
                        if j in gate_read_done and nt == 0:
                            pass
                        h = s.activation(gate_buf[:, sl], ps_a1[pp][:], AF.Sigmoid,
                                         bias=bias_col(role, j), scale=1.0)
                        evac_done.append(("S", p.inc(h, "S")))
                    elif role == "k":
                        p.wait(v, "T", mmv)
                        gsem, gval = evac_done[(c - 1) * 8 + nt]
                        p.wait(v, gsem, gval)
                        h = v.scalar_tensor_tensor(ktmp[:], ps_a1[pp][:], bias_col(role, j),
                                                   gate_buf[:, sl], ALU.add, ALU.mult)
                        fence(h, "V", v)
                        h = v.tensor_scalar(ktmp2[:], ktmp[:], 0.0, None, ALU.min)
                        vv = p.inc(h, "V")
                        p.wait(s, "V", vv)
                        h = s.activation(etile[:], ktmp2[:], AF.Exp, bias=0.0, scale=1.0)
                        sv = p.inc(h, "S")
                        p.wait(v, "S", sv)
                        h = v.scalar_tensor_tensor(kptile[:], ktmp[:], 0.0, etile[:],
                                                   ALU.max, ALU.add)
                        fence(h, "V", v)
                        h = split2(obhi[par][:, sl], oblo[par][:, sl], kptile[:])
                        evac_done.append(("V", p.inc(h, "V")))
                    elif role == "q":
                        p.wait(v, "T", mmv)
                        h = v.tensor_scalar(ktmp2[:], ps_a1[pp][:], bias_col(role, j), 0.0,
                                            ALU.add, ALU.min)
                        vv = p.inc(h, "V")
                        p.wait(s, "V", vv)
                        h = s.activation(etile[:], ktmp2[:], AF.Exp, bias=0.0, scale=1.0)
                        sv = p.inc(h, "S")
                        h = v.tensor_scalar(ktmp[:], ps_a1[pp][:], bias_col(role, j), 0.0,
                                            ALU.add, ALU.max)
                        fence(h, "V", v)
                        p.wait(v, "S", sv)
                        h = v.tensor_tensor(kptile[:], ktmp[:], etile[:], ALU.add)
                        fence(h, "V", v)
                        h = split2(obhi[par][:, sl], oblo[par][:, sl], kptile[:])
                        evac_done.append(("V", p.inc(h, "V")))
                    else:
                        p.wait(v, "T", mmv)
                        h = v.tensor_scalar(ktmp[:], ps_a1[pp][:], bias_col(role, j), None,
                                            ALU.add)
                        fence(h, "V", v)
                        h = split2(obhi[par][:, sl], oblo[par][:, sl], ktmp[:])
                        evac_done.append(("V", p.inc(h, "V")))
                    n_uses += 1
                if role != "gate":
                    shi, slo = ostore[role]
                    sem, val = evac_done[-1]
                    p.wait(g, sem, val)
                    h = g.dma_start(out=shi[j * 128:(j + 1) * 128, :], in_=obhi[par][:])
                    p.inc(h, "qout", 16)
                    h = g.dma_start(out=slo[j * 128:(j + 1) * 128, :], in_=oblo[par][:])
                    ob_dma[c] = p.inc(h, "qout", 16)
                else:
                    gate_read_done[j] = None
            phase_barrier()
        st_a01.close()

        SW = 1032
        with ExitStack() as ph:
            S_sb = sbuf(ph, "S_sb", [128, NDK * SW], F32)
            Shi = sbuf(ph, "Shi", [128, NDK * SW], BF16)
            Slo = sbuf(ph, "Slo", [128, NDK * SW], BF16)
            masks = sbuf(ph, "a2mask", [128, 4 * 512], F32)
            qfhi = sbuf(ph, "a2qfh", [128, NDK * 512], BF16)
            qflo = sbuf(ph, "a2qfl", [128, NDK * 512], BF16)
            kfhi = sbuf(ph, "a2kfh", [128, NDK * 512], BF16)
            kflo = sbuf(ph, "a2kfl", [128, NDK * 512], BF16)
            kthi = sbuf(ph, "a2kth", [128, 4 * 1024], BF16)
            ktlo = sbuf(ph, "a2ktl", [128, 4 * 1024], BF16)
            vthi = sbuf(ph, "a2vth", [128, 4 * 1024], BF16)
            vtlo = sbuf(ph, "a2vtl", [128, 4 * 1024], BF16)
            amhi = sbuf(ph, "a2amh", [128, 4 * 512], BF16)
            amlo = sbuf(ph, "a2aml", [128, 4 * 512], BF16)
            am32 = sbuf(ph, "a2am32", [128, 512], F32)
            numst = sbuf(ph, "a2num", [128, NDK * 512], F32)
            denst = sbuf(ph, "a2den", [1, 512], F32)
            psA = psum(ph, "psA", [128, 512], F32)
            psN = [psum(ph, f"psN{i}", [128, 512], F32) for i in range(2)]
            psS = [psum(ph, f"psS{i}", [128, 512], F32) for i in range(2)]
            psD = psum(ph, "psD", [1, 512], F32)
            psK = psum(ph, "psK", [128, 1], F32)

            for si in range(4):
                h = v.tensor_scalar(masks[:, si * 512:(si + 1) * 512], iota_f[:], pvec_f[:],
                                    float(1 - si * 128), ALU.subtract, ALU.add)
                fence(h, "V", v)
                h = v.tensor_scalar(masks[:, si * 512:(si + 1) * 512],
                                    masks[:, si * 512:(si + 1) * 512], 0.0, 1.0, ALU.max, ALU.min)
                fence(h, "V", v)

            state_T_done = 0
            s_split_done = 0
            num_dma = 0
            psn_evac = []
            pss_evac = []
            psd_evac = 0
            psk_evac = 0
            psa_evac = 0
            n_psn = 0
            n_pss = 0

            for c in range(NCA):
                base = c * CA
                if c > 0:
                    p.wait(q, "T", state_T_done)
                h = q.dma_start(out=qfhi[:].rearrange("p (a n) -> p a n", a=NDK),
                                in_=s_qhi[:, base:base + CA].rearrange("(a p) n -> p a n", p=128))
                p.inc(h, "qin", 16)
                h = q.dma_start(out=qflo[:].rearrange("p (a n) -> p a n", a=NDK),
                                in_=s_qlo[:, base:base + CA].rearrange("(a p) n -> p a n", p=128))
                p.inc(h, "qin", 16)
                h = q.dma_start(out=kfhi[:].rearrange("p (a n) -> p a n", a=NDK),
                                in_=s_khi[:, base:base + CA].rearrange("(a p) n -> p a n", p=128))
                p.inc(h, "qin", 16)
                h = q.dma_start(out=kflo[:].rearrange("p (a n) -> p a n", a=NDK),
                                in_=s_klo[:, base:base + CA].rearrange("(a p) n -> p a n", p=128))
                p.inc(h, "qin", 16)
                for si in range(4):
                    cs = slice(base + si * 128, base + (si + 1) * 128)
                    h = q.dma_start(out=kthi[:, si * 1024:(si + 1) * 1024], in_=s_khi[:, cs],
                                    transpose=True)
                    p.inc(h, "qin", 16)
                    h = q.dma_start(out=ktlo[:, si * 1024:(si + 1) * 1024], in_=s_klo[:, cs],
                                    transpose=True)
                    p.inc(h, "qin", 16)
                    h = q.dma_start(out=vthi[:, si * 1024:(si + 1) * 1024], in_=s_vhi[:, cs],
                                    transpose=True)
                    p.inc(h, "qin", 16)
                    h = q.dma_start(out=vtlo[:, si * 1024:(si + 1) * 1024], in_=s_vlo[:, cs],
                                    transpose=True)
                    p.inc(h, "qin", 16)
                lds = p.val["qin"]

                p.wait(t, "qin", lds)
                for si in range(4):
                    if si > 0 or c > 0:
                        p.wait(t, "V", psa_evac)
                    for kc in range(NDK):
                        h = mm3(psA[:],
                                kfhi[:, kc * 512 + si * 128: kc * 512 + (si + 1) * 128],
                                kflo[:, kc * 512 + si * 128: kc * 512 + (si + 1) * 128],
                                qfhi[:, kc * 512:(kc + 1) * 512],
                                qflo[:, kc * 512:(kc + 1) * 512],
                                start=(kc == 0), stop=(kc == NDK - 1))
                    mmv = p.inc(h, "T")
                    p.wait(v, "T", mmv)
                    h = v.tensor_tensor(am32[:], psA[:], masks[:, si * 512:(si + 1) * 512],
                                        ALU.mult)
                    fence(h, "V", v)
                    h = split2(amhi[:, si * 512:(si + 1) * 512],
                               amlo[:, si * 512:(si + 1) * 512], am32[:])
                    psa_evac = p.inc(h, "V")

                if c > 0:
                    p.wait(t, "V", s_split_done)
                if c > 0:
                    p.wait(v, "qout", num_dma)
                for dvc in range(NDK):
                    pp = n_psn % 2
                    if n_psn >= 2:
                        p.wait(t, "V", psn_evac[n_psn - 2])
                    first = True
                    if c > 0:
                        for kc in range(NDK):
                            h = mm3(psN[pp][:],
                                    Shi[:, kc * SW + dvc * 128: kc * SW + (dvc + 1) * 128],
                                    Slo[:, kc * SW + dvc * 128: kc * SW + (dvc + 1) * 128],
                                    qfhi[:, kc * 512:(kc + 1) * 512],
                                    qflo[:, kc * 512:(kc + 1) * 512],
                                    start=(kc == 0), stop=False)
                        first = False
                    for si in range(4):
                        h = mm3(psN[pp][:],
                                vthi[:, si * 1024 + dvc * 128: si * 1024 + (dvc + 1) * 128],
                                vtlo[:, si * 1024 + dvc * 128: si * 1024 + (dvc + 1) * 128],
                                amhi[:, si * 512:(si + 1) * 512],
                                amlo[:, si * 512:(si + 1) * 512],
                                start=(first and si == 0), stop=(si == 3))
                    mmv = p.inc(h, "T")
                    p.wait(v, "T", mmv)
                    h = v.tensor_copy(numst[:, dvc * 512:(dvc + 1) * 512], psN[pp][:])
                    psn_evac.append(p.inc(h, "V"))
                    n_psn += 1
                if psd_evac:
                    p.wait(t, "V", psd_evac)
                first = True
                if c > 0:
                    for kc in range(NDK):
                        h = mm3(psD[:],
                                Shi[:, kc * SW + 1024: kc * SW + 1025],
                                Slo[:, kc * SW + 1024: kc * SW + 1025],
                                qfhi[:, kc * 512:(kc + 1) * 512],
                                qflo[:, kc * 512:(kc + 1) * 512],
                                start=(kc == 0), stop=False)
                    first = False
                for si in range(4):
                    t.matmul(psD[:], ones_c16[:], amhi[:, si * 512:(si + 1) * 512],
                             start=(first and si == 0), stop=False)
                    h = t.matmul(psD[:], ones_c16[:], amlo[:, si * 512:(si + 1) * 512],
                                 start=False, stop=(si == 3))
                mmv = p.inc(h, "T")
                p.wait(v, "T", mmv)
                h = v.tensor_copy(denst[:], psD[:])
                psd_evac = p.inc(h, "V")

                p.wait(g, "V", psn_evac[-1])
                p.wait(g, "V", psd_evac)
                h = g.dma_start(out=s_num[:, base:base + CA].rearrange("(a p) n -> p a n", p=128),
                                in_=numst[:].rearrange("p (a n) -> p a n", a=NDK))
                p.inc(h, "qout", 16)
                h = g.dma_start(out=s_den[0:1, base:base + CA], in_=denst[:])
                num_dma = p.inc(h, "qout", 16)

                for kc in range(NDK):
                    for nt2 in range(2):
                        pp = n_pss % 2
                        if n_pss >= 2:
                            p.wait(t, "V", pss_evac[n_pss - 2])
                        for si in range(4):
                            h = mm3(psS[pp][:],
                                    kthi[:, si * 1024 + kc * 128: si * 1024 + (kc + 1) * 128],
                                    ktlo[:, si * 1024 + kc * 128: si * 1024 + (kc + 1) * 128],
                                    vthi[:, si * 1024 + nt2 * 512: si * 1024 + (nt2 + 1) * 512],
                                    vtlo[:, si * 1024 + nt2 * 512: si * 1024 + (nt2 + 1) * 512],
                                    start=(si == 0), stop=(si == 3))
                        mmv = p.inc(h, "T")
                        p.wait(v, "T", mmv)
                        dst = S_sb[:, kc * SW + nt2 * 512: kc * SW + (nt2 + 1) * 512]
                        if c == 0:
                            h = v.tensor_copy(dst, psS[pp][:])
                        else:
                            h = v.tensor_tensor(dst, dst, psS[pp][:], ALU.add)
                        pss_evac.append(p.inc(h, "V"))
                        n_pss += 1
                    if psk_evac:
                        p.wait(t, "V", psk_evac)
                    for si in range(4):
                        t.matmul(psK[:], kthi[:, si * 1024 + kc * 128: si * 1024 + (kc + 1) * 128],
                                 ones_c16[:], start=(si == 0), stop=False)
                        h = t.matmul(psK[:], ktlo[:, si * 1024 + kc * 128: si * 1024 + (kc + 1) * 128],
                                     ones_c16[:], start=False, stop=(si == 3))
                    mmv = p.inc(h, "T")
                    state_T_done = mmv
                    p.wait(v, "T", mmv)
                    dstk = S_sb[:, kc * SW + 1024: kc * SW + 1025]
                    if c == 0:
                        h = v.tensor_copy(dstk, psK[:])
                    else:
                        h = v.tensor_tensor(dstk, dstk, psK[:], ALU.add)
                    psk_evac = p.inc(h, "V")

                h = split2(Shi[:], Slo[:], S_sb[:])
                s_split_done = p.inc(h, "V")

            if dbg:
                p.wait(g, "V", s_split_done)
                h = g.dma_start(out=t_S[:, :], in_=S_sb[:])
                p.inc(h, "qout", 16)

            p.wait(v, "qout")
            h = v.tensor_scalar(S_sb[:], S_sb[:], slot(SL_HINV), None, ALU.mult)
            vv = p.inc(h, "V")
            p.wait(g, "V", vv)
            h = g.dma_start(out=cc_s_in[:, :], in_=S_sb[:])
            p.inc(h, "qout", 16)
            p.wait(g, "qout")
            h = g.collective_compute("AllReduce", ALU.add, replica_groups=GRP_PAIR,
                                     ins=[cc_s_in.opt()], outs=[cc_s_out.opt()])
            p.inc(h, "CC")
            phase_barrier()

        st_ffn = ExitStack()
        xqfm_hi = sbuf(st_ffn, "xqfm_hi", [128, NDK * T], BF16)
        xqfm_lo = sbuf(st_ffn, "xqfm_lo", [128, NDK * T], BF16)
        with ExitStack() as ph:
            xt = [sbuf(ph, f"xq_x{i}", [128, D], F32) for i in range(2)]
            xqt = sbuf(ph, "xq_t", [128, D], F32)
            xqhi_t = sbuf(ph, "xq_hi", [128, D], BF16)
            xqlo_t = sbuf(ph, "xq_lo", [128, D], BF16)
            ps_thi = psum(ph, "ps_xqh", [128, 1024], BF16)
            ps_tlo = psum(ph, "ps_xql", [128, 1024], BF16)
            NT0 = T // 128
            ld_done = {}
            for c in range(2):
                h = q.dma_start(out=xt[c][:], in_=s_xn[c * 128:(c + 1) * 128, :])
                ld_done[c] = p.inc(h, "qin", 16)
            for c in range(NT0):
                buf = xt[c % 2]
                p.wait(v, "qin", ld_done[c])
                h = v.tensor_scalar(xqt[:], buf[:], slot(SL_RS1), MAGIC, ALU.mult, ALU.add)
                buf_free = p.inc(h, "V")
                p.wait(v, "V", buf_free)
                if c + 2 < NT0:
                    p.wait(q, "V", buf_free)
                    h = q.dma_start(out=xt[c % 2][:], in_=s_xn[(c + 2) * 128:(c + 3) * 128, :])
                    ld_done[c + 2] = p.inc(h, "qin", 16)
                h = v.tensor_scalar(xqt[:], xqt[:], MAGIC, -32768.0, ALU.subtract, ALU.max)
                fence(h, "V", v)
                h = v.tensor_scalar(xqt[:], xqt[:], 32767.0, None, ALU.min)
                fence(h, "V", v)
                h = split2(xqhi_t[:], xqlo_t[:], xqt[:])
                vv = p.inc(h, "V")
                p.wait(t, "V", vv)
                for dc in range(NDK):
                    t.transpose(ps_thi[:, dc * 128:(dc + 1) * 128], xqhi_t[:, dc * 128:(dc + 1) * 128],
                                iden16[:])
                    h = t.transpose(ps_tlo[:, dc * 128:(dc + 1) * 128], xqlo_t[:, dc * 128:(dc + 1) * 128],
                                    iden16[:])
                mv = p.inc(h, "T")
                p.wait(v, "T", mv)
                for dc in range(NDK):
                    v.tensor_copy(xqfm_hi[:, dc * T + c * 128: dc * T + (c + 1) * 128],
                                  ps_thi[:, dc * 128:(dc + 1) * 128])
                    h = v.tensor_copy(xqfm_lo[:, dc * T + c * 128: dc * T + (c + 1) * 128],
                                      ps_tlo[:, dc * 128:(dc + 1) * 128])
                vv = p.inc(h, "V")
                p.wait(t, "V", vv)
            phase_barrier()

        with ExitStack() as ph:
            wtmp = [sbuf(ph, f"f2w{i}", [128, 3 * D], F32) for i in range(2)]
            blkm = sbuf(ph, "f2blk", [128, 96], F32)
            sblk = sbuf(ph, "f2s", [128, 96], F32)
            s1s = sbuf(ph, "f2s1s", [128, 96], F32)
            rqb = sbuf(ph, "f2rq", [128, 96], F32)
            t2 = sbuf(ph, "f2t2", [128, 3 * D], F32)
            wdhi = sbuf(ph, "f2wh", [128, 3 * D], BF16)
            wdlo = sbuf(ph, "f2wl", [128, 3 * D], BF16)
            wThi = sbuf(ph, "f2wth", [128, NDK * 128], BF16)
            wTlo = sbuf(ph, "f2wtl", [128, NDK * 128], BF16)
            stg = [sbuf(ph, f"f2stg{i}", [128, 512], F32) for i in range(2)]
            red2 = sbuf(ph, "f2red", [128, 1], F32)
            ps_tw = [psum(ph, f"ps_tw{i}", [128, 1024], BF16) for i in range(2)]
            ps_f = [psum(ph, f"ps_f{i}", [128, 512], F32) for i in range(2)]

            sched = []
            for j in range(DI // 128):
                sched.append(("up", j))
                sched.append(("gate", j))
                if j < NDK:
                    sched.append(("wdown", j))

            def src_ap(role, j):
                if role == "up":
                    return fw_up[j * 128:(j + 1) * 128, :]
                if role == "gate":
                    return fw_gate[j * 128:(j + 1) * 128, :]
                return fw_down[j * 128:(j + 1) * 128, :]

            ld_w = {}
            wfree = {}
            stg_dma = {}
            maccol = {"up": 1, "gate": 2}
            n_uses = 0
            evac_done = []

            for c in range(2):
                role, j = sched[c]
                W = D if role != "wdown" else 3 * D
                h = q.dma_start(out=wtmp[c][:, 0:W], in_=src_ap(role, j))
                ld_w[c] = p.inc(h, "qin", 16)

            for c, (role, j) in enumerate(sched):
                par = c % 2
                W = D if role != "wdown" else 3 * D
                NB = W // 32
                p.wait(v, "qin", ld_w[c])
                h = v.tensor_reduce(blkm[:, 0:NB], wtmp[par][:, 0:W].rearrange("p (g b) -> p g b", b=32),
                                    AX.X, ALU.max, apply_absolute_value=True)
                fence(h, "V", v)
                h = v.tensor_scalar(sblk[:, 0:NB], blkm[:, 0:NB], 1.0 / 7.0, 1e-12, ALU.mult, ALU.max)
                fence(h, "V", v)
                if role != "wdown":
                    h = v.tensor_scalar(s1s[:, 0:NB], sblk[:, 0:NB], slot(SL_S1), None, ALU.mult)
                    fence(h, "V", v)
                h = v.reciprocal(rqb[:, 0:NB], sblk[:, 0:NB])
                fence(h, "V", v)
                h = v.tensor_tensor(t2[:, 0:W].rearrange("p (g b) -> p g b", b=32),
                                    wtmp[par][:, 0:W].rearrange("p (g b) -> p g b", b=32),
                                    rqb[:, 0:NB].unsqueeze(-1).broadcast_to([128, NB, 32]), ALU.mult)
                wfree[c] = p.inc(h, "V")
                p.wait(v, "V", wfree[c])
                if c + 2 < len(sched):
                    role2, j2 = sched[c + 2]
                    W2 = D if role2 != "wdown" else 3 * D
                    p.wait(q, "V", wfree[c])
                    h = q.dma_start(out=wtmp[par][:, 0:W2], in_=src_ap(role2, j2))
                    ld_w[c + 2] = p.inc(h, "qin", 16)
                h = v.tensor_scalar(t2[:, 0:W], t2[:, 0:W], MAGIC, MAGIC, ALU.add, ALU.subtract)
                fence(h, "V", v)
                h = v.tensor_scalar(t2[:, 0:W], t2[:, 0:W], -8.0, 7.0, ALU.max, ALU.min)
                fence(h, "V", v)
                sc = s1s if role != "wdown" else sblk
                h = v.tensor_tensor(t2[:, 0:W].rearrange("p (g b) -> p g b", b=32),
                                    t2[:, 0:W].rearrange("p (g b) -> p g b", b=32),
                                    sc[:, 0:NB].unsqueeze(-1).broadcast_to([128, NB, 32]), ALU.mult)
                fence(h, "V", v)
                h = split2(wdhi[:, 0:W], wdlo[:, 0:W], t2[:, 0:W])
                spl = p.inc(h, "V")

                if role == "wdown":
                    p.wait(g, "V", spl)
                    h = g.dma_start(out=s_wdthi[j * 128:(j + 1) * 128, :], in_=wdhi[:, 0:W])
                    p.inc(h, "qout", 16)
                    h = g.dma_start(out=s_wdtlo[j * 128:(j + 1) * 128, :], in_=wdlo[:, 0:W])
                    p.inc(h, "qout", 16)
                    continue

                p.wait(t, "V", spl)
                for hc in range(NDK):
                    t.transpose(ps_tw[0][:, hc * 128:(hc + 1) * 128],
                                wdhi[:, hc * 128:(hc + 1) * 128], iden16[:])
                    h = t.transpose(ps_tw[1][:, hc * 128:(hc + 1) * 128],
                                    wdlo[:, hc * 128:(hc + 1) * 128], iden16[:])
                mv = p.inc(h, "T")
                p.wait(v, "T", mv)
                for hc in range(NDK):
                    v.tensor_copy(wThi[:, hc * 128:(hc + 1) * 128], ps_tw[0][:, hc * 128:(hc + 1) * 128])
                    h = v.tensor_copy(wTlo[:, hc * 128:(hc + 1) * 128], ps_tw[1][:, hc * 128:(hc + 1) * 128])
                wt_done = p.inc(h, "V")
                p.wait(t, "V", wt_done)

                dst = s_upraw if role == "up" else s_gateraw
                for nt in range(8):
                    pp = n_uses % 2
                    if n_uses >= 2:
                        sem, val = evac_done[n_uses - 2]
                        p.wait(t, sem, val)
                    for hc in range(NDK):
                        h = mm3(ps_f[pp][:],
                                wThi[:, hc * 128:(hc + 1) * 128],
                                wTlo[:, hc * 128:(hc + 1) * 128],
                                xqfm_hi[:, hc * T + nt * 512: hc * T + (nt + 1) * 512],
                                xqfm_lo[:, hc * T + nt * 512: hc * T + (nt + 1) * 512],
                                start=(hc == 0), stop=(hc == NDK - 1))
                    mmv = p.inc(h, "T")
                    p.wait(v, "T", mmv)
                    key = n_uses - 2
                    if key in stg_dma:
                        p.wait(v, "qout", stg_dma[key])
                    h = v.tensor_copy(stg[pp][:], ps_f[pp][:])
                    fence(h, "V", v)
                    h = v.tensor_reduce(red2[:], stg[pp][:], AX.X, ALU.max, apply_absolute_value=True)
                    fence(h, "V", v)
                    mc = maccol[role]
                    h = v.tensor_tensor(macc[:, mc:mc + 1], macc[:, mc:mc + 1], red2[:], ALU.max)
                    evac_done.append(("V", p.inc(h, "V")))
                    p.wait(g, "V", evac_done[-1][1])
                    h = g.dma_start(out=dst[j * 128:(j + 1) * 128, nt * 512:(nt + 1) * 512],
                                    in_=stg[pp][:])
                    stg_dma[n_uses] = p.inc(h, "qout", 16)
                    n_uses += 1

            vv = part_max_to_col(0, 1)
            vv = part_max_to_col(1, 2)
            p.wait(g, "V", vv)
            h = g.dma_start(out=cc23_in[:, :], in_=scrow[0:1, 0:16])
            p.inc(h, "qout", 16)
            p.wait(g, "qout")
            h = g.collective_compute("AllReduce", ALU.max, replica_groups=GRP_ALL,
                                     ins=[cc23_in.opt()], outs=[cc23_out.opt()])
            p.inc(h, "CC")
            phase_barrier()
        st_ffn.close()

        p.wait(q, "CC")
        with nc.sbuf_tensor("m23row", [1, 16], F32) as mrow:
            p.wait(g, "CC")
            h = g.dma_start(out=mrow[:], in_=cc23_out[:, :])
            lv = p.inc(h, "qin", 16)
            p.wait(t, "qin", lv)
            h = t.matmul(ps_red[:, 0:16], ones_r32[:], mrow[:], start=True, stop=True)
            mv = p.inc(h, "T")
            p.wait(v, "T", mv)
            v.tensor_copy(slot(SL_S2), ps_red[:, 0:1])
            h = v.tensor_copy(slot(SL_S3), ps_red[:, 1:2])
            fence(h, "V", v)
            h = v.tensor_scalar(slot(SL_S2), slot(SL_S2), 1.0 / 32767.0, 1e-12, ALU.mult, ALU.max)
            fence(h, "V", v)
            h = v.reciprocal(slot(SL_RS2), slot(SL_S2))
            fence(h, "V", v)
            h = v.tensor_scalar(slot(SL_S3), slot(SL_S3), 1.0 / 32767.0, 1e-12, ALU.mult, ALU.max)
            fence(h, "V", v)
            h = v.reciprocal(slot(SL_RS3), slot(SL_S3))
            fence(h, "V", v)
            phase_barrier()

        if dbg:
            h = v.tensor_copy(dbg_sb[:, 48:64], slots[:])
            vv = p.inc(h, "V")
            p.wait(g, "V", vv)
            h = g.dma_start(out=t_dbg[:, 48:64], in_=dbg_sb[:, 48:64])
            p.inc(h, "qout", 16)

        SW = 1032
        with ExitStack() as ph:
            SRhi = sbuf(ph, "SRhi", [128, NDK * SW], BF16)
            SRlo = sbuf(ph, "SRlo", [128, NDK * SW], BF16)
            wphi = sbuf(ph, "wphi", [128, NDK * D], BF16)
            wplo = sbuf(ph, "wplo", [128, NDK * D], BF16)
            bpb = sbuf(ph, "bpb", [128, D], F32)
            with ExitStack() as ph2:
                SR = sbuf(ph2, "SR", [128, NDK * SW], F32)
                wp32 = sbuf(ph2, "wp32", [128, D], F32)
                bprow = sbuf(ph2, "bprow", [1, D], F32)
                p.wait(q, "CC")
                h = q.dma_start(out=SR[:], in_=cc_s_out[:, :])
                lv = p.inc(h, "qin", 16)
                p.wait(v, "qin", lv)
                h = v.tensor_scalar(SR[:], SR[:], hfl[:], None, ALU.mult)
                fence(h, "V", v)
                h = split2(SRhi[:], SRlo[:], SR[:])
                p.inc(h, "V")
                for kc in range(NDK):
                    h = q.dma_start(out=wp32[:], in_=w_proj[kc * 128:(kc + 1) * 128, :])
                    lv = p.inc(h, "qin", 16)
                    p.wait(v, "qin", lv)
                    h = split2(wphi[:, kc * D:(kc + 1) * D], wplo[:, kc * D:(kc + 1) * D], wp32[:])
                    vv = p.inc(h, "V")
                    if kc + 1 < NDK:
                        p.wait(q, "V", vv)
                h = q.dma_start(out=bprow[:], in_=b_proj.unsqueeze(0))
                lv = p.inc(h, "qin", 16)
                p.wait(t, "qin", lv)
                for hf in range(2):
                    h = t.matmul(ps_red[:, 0:512], ones_r32[:], bprow[:, hf * 512:(hf + 1) * 512],
                                 start=True, stop=True)
                    mv = p.inc(h, "T")
                    p.wait(v, "T", mv)
                    h = v.tensor_copy(bpb[:, hf * 512:(hf + 1) * 512], ps_red[:, 0:512])
                    vv = p.inc(h, "V")
                    p.wait(t, "V", vv)
                phase_barrier()

            with ExitStack() as ph3:
                numst = sbuf(ph3, "fnum", [128, NDK * 512], F32)
                denrow = sbuf(ph3, "fden", [1, 512], F32)
                qfhi = sbuf(ph3, "fqfh", [128, NDK * 512], BF16)
                qflo = sbuf(ph3, "fqfl", [128, NDK * 512], BF16)
                xrt = sbuf(ph3, "fxrt", [128, 4 * D], F32)
                ndt = sbuf(ph3, "fndt", [128, 512], F32)
                ndhi = sbuf(ph3, "fndh", [128, NDK * 512], BF16)
                ndlo = sbuf(ph3, "fndl", [128, NDK * 512], BF16)
                rstg = sbuf(ph3, "frst", [128, D], F32)
                psC = [psum(ph3, f"psC{i}", [128, 512], F32) for i in range(2)]
                psB = psum(ph3, "psB", [128, 512], F32)
                psD2 = psum(ph3, "psD2", [1, 512], F32)

                ld_prev_T = 0
                ld_prev_V = 0
                rstg_dma = 0
                for cf in range(NCA):
                    base = cf * CA
                    if cf > 0:
                        p.wait(q, "T", ld_prev_T)
                        p.wait(q, "V", ld_prev_V)
                    h = q.dma_start(out=numst[:].rearrange("p (a n) -> p a n", a=NDK),
                                    in_=s_num[:, base:base + CA].rearrange("(a p) n -> p a n", p=128))
                    p.inc(h, "qin", 16)
                    h = q.dma_start(out=denrow[:], in_=s_den[0:1, base:base + CA])
                    p.inc(h, "qin", 16)
                    h = q.dma_start(out=qfhi[:].rearrange("p (a n) -> p a n", a=NDK),
                                    in_=s_qhi[:, base:base + CA].rearrange("(a p) n -> p a n", p=128))
                    p.inc(h, "qin", 16)
                    h = q.dma_start(out=qflo[:].rearrange("p (a n) -> p a n", a=NDK),
                                    in_=s_qlo[:, base:base + CA].rearrange("(a p) n -> p a n", p=128))
                    p.inc(h, "qin", 16)
                    h = q.dma_start(out=xrt[:].rearrange("p (a n) -> p a n", a=4),
                                    in_=x_in[base:base + CA, :].rearrange("(a p) n -> p a n", p=128))
                    lds = p.inc(h, "qin", 16)
                    p.wait(t, "qin", lds)
                    psc_evac = []
                    for dvc in range(NDK):
                        pp = dvc % 2
                        if dvc >= 2:
                            p.wait(t, "V", psc_evac[dvc - 2])
                        for kc in range(NDK):
                            h = mm3(psC[pp][:],
                                    SRhi[:, kc * SW + dvc * 128: kc * SW + (dvc + 1) * 128],
                                    SRlo[:, kc * SW + dvc * 128: kc * SW + (dvc + 1) * 128],
                                    qfhi[:, kc * 512:(kc + 1) * 512],
                                    qflo[:, kc * 512:(kc + 1) * 512],
                                    start=(kc == 0), stop=(kc == NDK - 1))
                        mmv = p.inc(h, "T")
                        p.wait(v, "T", mmv)
                        dstn = numst[:, dvc * 512:(dvc + 1) * 512]
                        h = v.tensor_tensor(dstn, dstn, psC[pp][:], ALU.add)
                        psc_evac.append(p.inc(h, "V"))
                    for kc in range(NDK):
                        h = mm3(psD2[:],
                                SRhi[:, kc * SW + 1024: kc * SW + 1025],
                                SRlo[:, kc * SW + 1024: kc * SW + 1025],
                                qfhi[:, kc * 512:(kc + 1) * 512],
                                qflo[:, kc * 512:(kc + 1) * 512],
                                start=(kc == 0), stop=(kc == NDK - 1))
                    mmv = p.inc(h, "T")
                    p.wait(v, "T", mmv)
                    h = v.tensor_tensor(denrow[:], denrow[:], psD2[:], ALU.add)
                    fence(h, "V", v)
                    h = v.tensor_scalar(denrow[:], denrow[:], DEN_EPS, None, ALU.add)
                    fence(h, "V", v)
                    h = v.reciprocal(denrow[:], denrow[:])
                    dden = p.inc(h, "V")
                    p.wait(t, "V", dden)
                    h = t.matmul(psB[:], ones_r32[:], denrow[:], start=True, stop=True)
                    bden = p.inc(h, "T")
                    p.wait(v, "T", bden)
                    for dvc in range(NDK):
                        h = v.tensor_tensor(ndt[:], numst[:, dvc * 512:(dvc + 1) * 512], psB[:],
                                            ALU.mult)
                        fence(h, "V", v)
                        h = split2(ndhi[:, dvc * 512:(dvc + 1) * 512],
                                   ndlo[:, dvc * 512:(dvc + 1) * 512], ndt[:])
                        p.inc(h, "V")
                    nds = p.val["V"]
                    p.wait(t, "V", nds)
                    proj_evac = {0: None, 1: None}
                    for ts4 in range(4):
                        for nh in range(2):
                            pp = nh % 2
                            if proj_evac[pp] is not None:
                                p.wait(t, "V", proj_evac[pp])
                            for dvc in range(NDK):
                                h = mm3(psC[pp][:],
                                        ndhi[:, dvc * 512 + ts4 * 128: dvc * 512 + (ts4 + 1) * 128],
                                        ndlo[:, dvc * 512 + ts4 * 128: dvc * 512 + (ts4 + 1) * 128],
                                        wphi[:, dvc * D + nh * 512: dvc * D + (nh + 1) * 512],
                                        wplo[:, dvc * D + nh * 512: dvc * D + (nh + 1) * 512],
                                        start=(dvc == 0), stop=(dvc == NDK - 1))
                            mmv = p.inc(h, "T")
                            if cf > 0 or ts4 > 0 or nh > 0:
                                pass
                            p.wait(v, "T", mmv)
                            if rstg_dma:
                                p.wait(v, "qout", rstg_dma)
                            h = v.scalar_tensor_tensor(rstg[:, nh * 512:(nh + 1) * 512], psC[pp][:],
                                                       1.0, bpb[:, nh * 512:(nh + 1) * 512],
                                                       ALU.mult, ALU.add)
                            fence(h, "V", v)
                            h = v.tensor_tensor(rstg[:, nh * 512:(nh + 1) * 512],
                                                rstg[:, nh * 512:(nh + 1) * 512],
                                                xrt[:, ts4 * D + nh * 512: ts4 * D + (nh + 1) * 512],
                                                ALU.add)
                            vv = p.inc(h, "V")
                            proj_evac[pp] = vv
                        p.wait(g, "V", vv)
                        h = g.dma_start(out=s_r[base + ts4 * 128: base + (ts4 + 1) * 128, :],
                                        in_=rstg[:])
                        rstg_dma = p.inc(h, "qout", 16)
                    ld_prev_T = p.val["T"]
                    ld_prev_V = p.val["V"]
                phase_barrier()

        with ExitStack() as ph:
            upt = sbuf(ph, "q3up", [128, T], F32)
            gtt = sbuf(ph, "q3gt", [128, T], F32)
            sig = sbuf(ph, "q3sig", [128, T], F32)
            g2v = sbuf(ph, "q3g2", [128, T], F32)
            q16 = sbuf(ph, "q3i16", [128, T], I16)
            red2 = sbuf(ph, "q3red", [128, 1], F32)
            q16_dma = {}
            for j in range(DI // 128):
                rows = slice(j * 128, (j + 1) * 128)
                if j > 0:
                    p.wait(q, "V")
                    p.wait(q, "S")
                h = q.dma_start(out=upt[:], in_=s_upraw[rows, :])
                p.inc(h, "qin", 16)
                h = q.dma_start(out=gtt[:], in_=s_gateraw[rows, :])
                lv = p.inc(h, "qin", 16)
                p.wait(v, "qin", lv)
                h = v.tensor_scalar(upt[:], upt[:], slot(SL_RS2), MAGIC, ALU.mult, ALU.add)
                fence(h, "V", v)
                h = v.tensor_scalar(upt[:], upt[:], MAGIC, -32768.0, ALU.subtract, ALU.max)
                fence(h, "V", v)
                h = v.tensor_scalar(upt[:], upt[:], 32767.0, None, ALU.min)
                fence(h, "V", v)
                if (j - 1, 1) in q16_dma:
                    p.wait(v, "qout", q16_dma[(j - 1, 1)])
                h = v.tensor_copy(q16[:], upt[:])
                vv = p.inc(h, "V")
                p.wait(g, "V", vv)
                h = g.dma_start(out=s_upq[rows, :], in_=q16[:])
                q16_dma[(j, 0)] = p.inc(h, "qout", 16)
                h = v.tensor_scalar(gtt[:], gtt[:], slot(SL_RS3), MAGIC, ALU.mult, ALU.add)
                fence(h, "V", v)
                h = v.tensor_scalar(gtt[:], gtt[:], MAGIC, -32768.0, ALU.subtract, ALU.max)
                fence(h, "V", v)
                h = v.tensor_scalar(gtt[:], gtt[:], 32767.0, None, ALU.min)
                vv = p.inc(h, "V")
                p.wait(v, "V", vv)
                p.wait(v, "qout", q16_dma[(j, 0)])
                h = v.tensor_copy(q16[:], gtt[:])
                vv = p.inc(h, "V")
                p.wait(g, "V", vv)
                h = g.dma_start(out=s_gateq[rows, :], in_=q16[:])
                q16_dma[(j, 1)] = p.inc(h, "qout", 16)
                p.wait(s, "V", vv)
                h = s.activation(sig[:], gtt[:], AF.Sigmoid, bias=0.0, scale=slot(SL_S3))
                sv = p.inc(h, "S")
                p.wait(s, "S", sv)
                h = s.activation(sig[:], sig[:], AF.Copy, bias=MAGIC, scale=65536.0)
                sv = p.inc(h, "S")
                p.wait(v, "S", sv)
                h = v.tensor_scalar(sig[:], sig[:], MAGIC, 65535.0, ALU.subtract, ALU.min)
                fence(h, "V", v)
                h = v.tensor_tensor(g2v[:], gtt[:], sig[:], ALU.mult)
                fence(h, "V", v)
                h = v.tensor_reduce(red2[:], g2v[:], AX.X, ALU.max, apply_absolute_value=True)
                fence(h, "V", v)
                h = v.tensor_tensor(macc[:, 3:4], macc[:, 3:4], red2[:], ALU.max)
                p.inc(h, "V")
            vv = part_max_to_col(0, 3)
            p.wait(g, "V", vv)
            h = g.dma_start(out=cc4_in[:, :], in_=scrow[0:1, 0:16])
            p.inc(h, "qout", 16)
            p.wait(g, "qout")
            h = g.collective_compute("AllReduce", ALU.max, replica_groups=GRP_ALL,
                                     ins=[cc4_in.opt()], outs=[cc4_out.opt()])
            p.inc(h, "CC")
            phase_barrier()

        SL_SG2, SL_T1, SL_T2 = 11, 12, 13
        p.wait(q, "CC")
        with nc.sbuf_tensor("m4row", [1, 16], F32) as mrow:
            bcast_cc_row(cc4_out[:, :], mrow[:], slot(SL_S4), 1)
            h = v.tensor_scalar(slot(SL_SG2), slot(SL_S3), 1.0 / 65536.0, None, ALU.mult)
            fence(h, "V", v)
            h = v.tensor_tensor(slot(SL_S4), slot(SL_S4), slot(SL_SG2), ALU.mult)
            fence(h, "V", v)
            h = v.tensor_scalar(slot(SL_S4), slot(SL_S4), 1.0 / 32767.0, 1e-12, ALU.mult, ALU.max)
            fence(h, "V", v)
            h = v.reciprocal(slot(SL_T1), slot(SL_S4))
            fence(h, "V", v)
            h = v.tensor_tensor(slot(SL_C4), slot(SL_SG2), slot(SL_T1), ALU.mult)
            fence(h, "V", v)
            phase_barrier()

        with ExitStack() as ph:
            gq16 = sbuf(ph, "b3gq", [128, T], I16)
            uq16 = sbuf(ph, "b3uq", [128, T], I16)
            sig = sbuf(ph, "b3sig", [128, T], F32)
            g2v = sbuf(ph, "b3g2", [128, T], F32)
            ov = sbuf(ph, "b3ov", [128, T], F32)
            o16 = sbuf(ph, "b3o16", [128, T], I16)
            red2 = sbuf(ph, "b3red", [128, 1], F32)
            o16_dma = 0
            for j in range(DI // 128):
                rows = slice(j * 128, (j + 1) * 128)
                if j > 0:
                    p.wait(q, "V")
                    p.wait(q, "S")
                h = q.dma_start(out=gq16[:], in_=s_gateq[rows, :])
                p.inc(h, "qin", 16)
                h = q.dma_start(out=uq16[:], in_=s_upq[rows, :])
                lv = p.inc(h, "qin", 16)
                p.wait(s, "qin", lv)
                h = s.activation(sig[:], gq16[:], AF.Sigmoid, bias=0.0, scale=slot(SL_S3))
                sv = p.inc(h, "S")
                p.wait(s, "S", sv)
                h = s.activation(sig[:], sig[:], AF.Copy, bias=MAGIC, scale=65536.0)
                sv = p.inc(h, "S")
                p.wait(v, "S", sv)
                h = v.tensor_scalar(sig[:], sig[:], MAGIC, 65535.0, ALU.subtract, ALU.min)
                fence(h, "V", v)
                h = v.tensor_tensor(g2v[:], gq16[:], sig[:], ALU.mult)
                fence(h, "V", v)
                h = v.tensor_scalar(g2v[:], g2v[:], slot(SL_C4), MAGIC, ALU.mult, ALU.add)
                fence(h, "V", v)
                h = v.tensor_scalar(g2v[:], g2v[:], MAGIC, -32768.0, ALU.subtract, ALU.max)
                fence(h, "V", v)
                h = v.tensor_scalar(g2v[:], g2v[:], 32767.0, None, ALU.min)
                fence(h, "V", v)
                if o16_dma:
                    p.wait(v, "qout", o16_dma)
                h = v.tensor_copy(o16[:], g2v[:])
                vv = p.inc(h, "V")
                p.wait(g, "V", vv)
                h = g.dma_start(out=s_g2q[rows, :], in_=o16[:])
                o16_dma = p.inc(h, "qout", 16)
                h = v.tensor_tensor(ov[:], g2v[:], uq16[:], ALU.mult)
                fence(h, "V", v)
                h = v.tensor_reduce(red2[:], ov[:], AX.X, ALU.max, apply_absolute_value=True)
                fence(h, "V", v)
                h = v.tensor_tensor(macc[:, 4:5], macc[:, 4:5], red2[:], ALU.max)
                p.inc(h, "V")
            vv = part_max_to_col(0, 4)
            p.wait(g, "V", vv)
            h = g.dma_start(out=cc5_in[:, :], in_=scrow[0:1, 0:16])
            p.inc(h, "qout", 16)
            p.wait(g, "qout")
            h = g.collective_compute("AllReduce", ALU.max, replica_groups=GRP_ALL,
                                     ins=[cc5_in.opt()], outs=[cc5_out.opt()])
            p.inc(h, "CC")
            phase_barrier()

        p.wait(q, "CC")
        with nc.sbuf_tensor("m5row", [1, 16], F32) as mrow:
            bcast_cc_row(cc5_out[:, :], mrow[:], slot(SL_S5), 1)
            h = v.tensor_tensor(slot(SL_T2), slot(SL_S4), slot(SL_S2), ALU.mult)
            fence(h, "V", v)
            h = v.tensor_tensor(slot(SL_S5), slot(SL_S5), slot(SL_T2), ALU.mult)
            fence(h, "V", v)
            h = v.tensor_scalar(slot(SL_S5), slot(SL_S5), 1.0 / 32767.0, 1e-12, ALU.mult, ALU.max)
            fence(h, "V", v)
            h = v.reciprocal(slot(SL_T1), slot(SL_S5))
            fence(h, "V", v)
            h = v.tensor_tensor(slot(SL_C5), slot(SL_T2), slot(SL_T1), ALU.mult)
            fence(h, "V", v)
            phase_barrier()

        with ExitStack() as ph:
            wdthi = sbuf(ph, "p7wdh", [128, 24 * D], BF16)
            wdtlo = sbuf(ph, "p7wdl", [128, 24 * D], BF16)
            g2qt = sbuf(ph, "p7g2", [128, 24 * 512], I16)
            uqt = sbuf(ph, "p7uq", [128, 24 * 512], I16)
            rt = sbuf(ph, "p7rt", [128, 4 * D], F32)
            ot = sbuf(ph, "p7ot", [128, 512], F32)
            ohi = g2qt.bitcast(BF16)
            olo = uqt.bitcast(BF16)
            ystg = sbuf(ph, "p7y", [128, D], F32)
            psP = [psum(ph, f"p7ps{i}", [128, 512], F32) for i in range(2)]

            for ic in range(24):
                h = q.dma_start(out=wdthi[:, ic * D:(ic + 1) * D],
                                in_=s_wdthi[:, ic * 128:(ic + 1) * 128], transpose=True)
                p.inc(h, "qin", 16)
                h = q.dma_start(out=wdtlo[:, ic * D:(ic + 1) * D],
                                in_=s_wdtlo[:, ic * 128:(ic + 1) * 128], transpose=True)
                p.inc(h, "qin", 16)
            wd_lds = p.val["qin"]
            p.wait(t, "qin", wd_lds)

            ystg_dma = 0
            grp_T_done = 0
            grp_V_done = 0
            for gq in range(NCA):
                base = gq * CA
                if gq > 0:
                    p.wait(q, "T", grp_T_done)
                    p.wait(q, "V", grp_V_done)
                h = q.dma_start(out=g2qt[:].rearrange("p (a n) -> p a n", a=24),
                                in_=s_g2q[:, base:base + CA].rearrange("(a p) n -> p a n", p=128))
                p.inc(h, "qin", 16)
                h = q.dma_start(out=uqt[:].rearrange("p (a n) -> p a n", a=24),
                                in_=s_upq[:, base:base + CA].rearrange("(a p) n -> p a n", p=128))
                p.inc(h, "qin", 16)
                h = q.dma_start(out=rt[:].rearrange("p (a n) -> p a n", a=4),
                                in_=s_r[base:base + CA, :].rearrange("(a p) n -> p a n", p=128))
                lds = p.inc(h, "qin", 16)
                p.wait(v, "qin", lds)
                if gq > 0:
                    p.wait(v, "T", grp_T_done)
                o_done = {}
                for ic in range(24):
                    sl = slice(ic * 512, (ic + 1) * 512)
                    h = v.tensor_tensor(ot[:], g2qt[:, sl], uqt[:, sl], ALU.mult)
                    fence(h, "V", v)
                    h = v.tensor_scalar(ot[:], ot[:], slot(SL_C5), MAGIC, ALU.mult, ALU.add)
                    fence(h, "V", v)
                    h = v.tensor_scalar(ot[:], ot[:], MAGIC, -32768.0, ALU.subtract, ALU.max)
                    fence(h, "V", v)
                    h = v.tensor_scalar(ot[:], ot[:], 32767.0, None, ALU.min)
                    fence(h, "V", v)
                    h = split2(ohi[:, sl], olo[:, sl], ot[:])
                    o_done[ic] = p.inc(h, "V")
                p7_evac = {0: None, 1: None}
                first_group = True
                for tc in range(4):
                    for nh in range(2):
                        pp = nh
                        if p7_evac[pp] is not None:
                            p.wait(t, "V", p7_evac[pp])
                        for ic in range(24):
                            if first_group:
                                p.wait(t, "V", o_done[ic])
                            h = mm3(psP[pp][:],
                                    ohi[:, ic * 512 + tc * 128: ic * 512 + (tc + 1) * 128],
                                    olo[:, ic * 512 + tc * 128: ic * 512 + (tc + 1) * 128],
                                    wdthi[:, ic * D + nh * 512: ic * D + (nh + 1) * 512],
                                    wdtlo[:, ic * D + nh * 512: ic * D + (nh + 1) * 512],
                                    start=(ic == 0), stop=(ic == 23))
                        first_group = False
                        mmv = p.inc(h, "T")
                        p.wait(v, "T", mmv)
                        if ystg_dma:
                            p.wait(v, "qout", ystg_dma)
                        h = v.scalar_tensor_tensor(ystg[:, nh * 512:(nh + 1) * 512], psP[pp][:],
                                                   slot(SL_S5),
                                                   rt[:, tc * D + nh * 512: tc * D + (nh + 1) * 512],
                                                   ALU.mult, ALU.add)
                        vv = p.inc(h, "V")
                        p7_evac[pp] = vv
                    p.wait(g, "V", vv)
                    h = g.dma_start(out=y_out[base + tc * 128: base + (tc + 1) * 128, :], in_=ystg[:])
                    ystg_dma = p.inc(h, "qout", 16)
                grp_T_done = p.val["T"]
                grp_V_done = p.val["V"]
            phase_barrier()

        if dbg == 2:
            for nm, srcap in (("upraw", s_upraw[0:128, :]), ("upq", s_upq[0:128, :]),
                              ("gateq", s_gateq[0:128, :]), ("g2q", s_g2q[0:128, :]),
                              ("r", s_r[0:128, :]), ("num", s_num[0:128, :]),
                              ("wdh", s_wdthi[0:128, :])):
                h = g.dma_start(out=t_pr[nm][:, :], in_=srcap)
                p.inc(h, "qout", 16)
            p.wait(g, "qout")

        h = g.dma_start(out=t_slots[:, :], in_=slots[:])
        p.inc(h, "qout", 16)
        h = g.dma_start(out=t_macc[:, :], in_=macc[:])
        p.inc(h, "qout", 16)
        for i, cco in enumerate((cc1_out, cc23_out, cc4_out, cc5_out)):
            h = g.dma_start(out=t_cc[i:i + 1, :], in_=cco[:, :])
            p.inc(h, "qout", 16)

        h = t.matmul(ps_red[:, 0:16], ones_r32[:], scrow[0:1, 0:16], start=True, stop=True)
        p.inc(h, "T")
        h = v.engine_nop()
        p.inc(h, "V")
        h = s.copy(red1[:, :], red1[:, :])
        p.inc(h, "S")
        p.wait(g, "T")
        p.wait(g, "V")
        p.wait(g, "S")
        p.wait(g, "qin")
        p.wait(g, "qout")
        p.wait(g, "CC")

    return nc


def kernel(L0_attn_x, L0_attn_w_qkv, L0_attn_b_qkv, L0_attn_w_gate, L0_attn_b_gate,
           L0_attn_w_proj, L0_attn_b_proj, L0_attn_ln_g, L0_attn_ln_b,
           L0_ffn_w_gate, L0_ffn_w_up, L0_ffn_w_down):
    x = np.ascontiguousarray(np.asarray(L0_attn_x, dtype=np.float32))
    B, TT, DD = x.shape
    nc = build_nc()
    in_maps = []
    for core in range(8):
        b, hh = core // 2, core % 2
        in_maps.append({
            "x": np.ascontiguousarray(x[b, hh * T:(hh + 1) * T, :]),
            "w_qkv": np.asarray(L0_attn_w_qkv, dtype=np.float32),
            "b_qkv": np.asarray(L0_attn_b_qkv, dtype=np.float32),
            "w_gate": np.asarray(L0_attn_w_gate, dtype=np.float32),
            "b_gate": np.asarray(L0_attn_b_gate, dtype=np.float32),
            "w_proj": np.asarray(L0_attn_w_proj, dtype=np.float32),
            "b_proj": np.asarray(L0_attn_b_proj, dtype=np.float32),
            "ln_g": np.asarray(L0_attn_ln_g, dtype=np.float32),
            "ln_b": np.asarray(L0_attn_ln_b, dtype=np.float32),
            "fw_gate": np.asarray(L0_ffn_w_gate, dtype=np.float32),
            "fw_up": np.asarray(L0_ffn_w_up, dtype=np.float32),
            "fw_down": np.asarray(L0_ffn_w_down, dtype=np.float32),
            "hflag": np.full((128, 1), float(hh), dtype=np.float32),
        })
    res = run_bass_kernel_spmd(nc, in_maps, core_ids=list(range(8)))
    out = np.empty((B, TT, DD), dtype=np.float32)
    for core in range(8):
        b, hh = core // 2, core % 2
        out[b, hh * T:(hh + 1) * T, :] = res.results[core]["y"]
    return out
